# version 09
# baseline (speedup 1.0000x reference)
import numpy as np
import concourse.bass as bass
import concourse.mybir as mybir
from concourse.bass_utils import run_bass_kernel_spmd
from contextlib import ExitStack

F32 = mybir.dt.float32
BF16 = mybir.dt.bfloat16
FP16 = mybir.dt.float16
ALU = mybir.AluOpType
AF = mybir.ActivationFunctionType
AX = mybir.AxisListType

NCORES = 8
N, D, I = 2048, 1024, 2816
NT = N // 128
KC = D // 128
IC = I // 128
RND_C = 12582912.0
RND_HC = 6291456.0
PE_TAPS = list(range(1, 15))
DVE_TAPS = list(range(15, 19))
POOL_TAPS = list(range(19, 21))
PAD = 32
QW = 1024
DEBUG_TAPS = False


class Sched:

    ENGS = ("pe", "dve", "act", "pool", "sp")
    NPOOL = {"act": 12, "pool": 8, "sp": 20}

    def __init__(self, nc):
        self.nc = nc
        self.items = {e: [] for e in self.ENGS}
        self.count = {}
        self.sems = {}
        self.rot = {e: 0 for e in self.NPOOL}

    def make_sems(self, es):
        for e in self.ENGS:
            self.sems[e] = es.enter_context(self.nc.semaphore(f"sem_{e}"))
            self.count[e] = 0
        for e, np_ in self.NPOOL.items():
            for i in range(np_):
                q = f"{e}q{i}"
                self.sems[q] = es.enter_context(self.nc.semaphore(f"sem_{q}"))
                self.count[q] = 0

    def op(self, eng, fn, deps=()):
        self.count[eng] += 1
        tok = (eng, self.count[eng])
        self.items[eng].append(
            (fn, tuple(d for d in deps if d is not None), tok, 1, eng))
        return tok

    def dma(self, eng, out, in_, deps=(), **kw):
        assert eng in self.NPOOL
        q = f"{eng}q{self.rot[eng]}"
        self.rot[eng] = (self.rot[eng] + 1) % self.NPOOL[eng]
        prev = self.count[q]
        deps = tuple(d for d in deps if d is not None)
        if prev > 0:
            deps = deps + ((q, prev),)
        self.count[q] += 16
        tok = (q, self.count[q])
        self.items[eng].append(
            (lambda e: e.dma_start(out=out, in_=in_, **kw), deps, tok, 16, q))
        return tok

    def finalize(self):
        self.prior = {}
        for eng in self.ENGS:
            run = {}
            for (fn, deps, tok, inc, sem) in self.items[eng]:
                self.prior[tok] = dict(run)
                run[sem] = tok[1]

    def replay(self, eng, e):
        waited = {}
        for (fn, deps, tok, inc, sem) in self.items[eng]:
            want = {}
            for (s, v) in deps:
                if s == eng:
                    v = min(v, self.prior[tok].get(s, 0))
                if v > want.get(s, 0):
                    want[s] = v
            for s, v in want.items():
                if v > waited.get(s, 0):
                    e.wait_ge(self.sems[s], v)
                    waited[s] = v
            inst = fn(e)
            if inst is not None:
                inst.then_inc(self.sems[sem], inc)

    def barrier_token(self, eng, dma=False):
        return (eng, self.count[eng])

    def all_dma_barrier(self, eng):
        return tuple((f"{eng}q{i}", self.count[f"{eng}q{i}"])
                     for i in range(self.NPOOL[eng])
                     if self.count[f"{eng}q{i}"] > 0)


class Arena:
    def __init__(self, nc, total_bytes):
        self.nc = nc
        self.total = total_bytes
        self.handle = None
        self.named = []

    def reserve(self, es):
        self.handle = es.enter_context(
            self.nc.sbuf_tensor("arena", [128, self.total], mybir.dt.uint8))

    def at(self, name, offset, shape, dtype):
        per_part = int(np.prod(shape[1:])) * mybir.dt.size(dtype)
        assert offset % 32 == 0, (name, offset)
        assert offset + per_part <= self.total, (name, offset, per_part)
        h = self.nc.virtual_sbuf_tensor(name, shape, dtype)
        self.named.append((h, offset))
        return h

    def finalize(self):
        base = self.nc.lookup_mloc(self.handle).addr
        for h, off in self.named:
            self.nc._tensor(h.name, h.shape, h.dtype, type="SB")
            mloc = self.nc.lookup_mloc(h)
            mloc.allocated = not self.nc.target_bir_lowering
            mloc.addr = base + off


def emit_quant(sched, bufs, src_ap, q_out_ap, width, src_tok, extra_q_deps=(),
               pool_combines=False, pool_q=None, prev1=None, prev2=None,
               defer_tail=False):
    nb = width // 16
    amax, s_t, rs_t = bufs["amax"], bufs["s"], bufs["rs"]
    u, b5 = bufs["u"], bufs["b5"]
    b1, b2, b3, b4 = bufs["b1"], bufs["b2"], bufs["b3"], bufs["b4"]
    s1v, s2v = bufs["s1v"], bufs["s2v"]
    ceng = "pool" if pool_combines else "dve"
    qeng = ceng if pool_q is None else ("pool" if pool_q else "dve")
    p1 = prev1 or {}
    p2 = prev2 or {}

    def g(d, *keys):
        return tuple(d[k] for k in keys if d.get(k) is not None)

    src_deps = tuple(src_tok) if isinstance(src_tok, (tuple, list)) and \
        src_tok and isinstance(src_tok[0], tuple) else (src_tok,)
    T = {}
    src3 = src_ap.rearrange("p (b k) -> p b k", k=16)
    T["am"] = sched.op("dve", lambda e: e.tensor_reduce(
        amax[:, :nb], src3, AX.X, ALU.max, apply_absolute_value=True),
        deps=src_deps + g(p2, "s"))
    T["s"] = sched.op("dve", lambda e: e.tensor_scalar(
        s_t[:, :nb], amax[:, :nb], 1.0 / 6.0, 1e-12, ALU.mult, ALU.max),
        deps=(T["am"],) + g(p2, "q"))
    T["rs"] = sched.op("dve", lambda e: e.reciprocal(rs_t[:, :nb], s_t[:, :nb]),
                       deps=(T["s"],) + g(p2, "u"))
    rs_b = rs_t[:, :nb].unsqueeze(2).broadcast_to([128, nb, 16])
    s_b = s_t[:, :nb].unsqueeze(2).broadcast_to([128, nb, 16])
    u3 = u[:, :width].rearrange("p (b k) -> p b k", k=16)
    T["u"] = sched.op("dve", lambda e: e.tensor_tensor(u3, src3, rs_b, ALU.mult),
                      deps=(T["rs"],) + g(p1, "r", "s2"))
    uf = u[:, :width]
    T["rh"] = sched.op("dve", lambda e: e.tensor_scalar(
        b5[:, :width], uf, RND_HC, RND_HC, ALU.add, ALU.subtract),
        deps=(T["u"],) + g(p1, "a2", "c2"))
    T["a2"] = sched.op("dve", lambda e: e.tensor_scalar(
        b1[:, :width], b5[:, :width], -2.0, 2.0, ALU.max, ALU.min),
        deps=(T["rh"],) + g(p1, "g", "q"))
    T["r"] = sched.op("dve", lambda e: e.tensor_scalar(
        b5[:, :width], uf, RND_C, RND_C, ALU.add, ALU.subtract),
        deps=(T["a2"],))
    T["b1"] = sched.op("dve", lambda e: e.tensor_scalar(
        b2[:, :width], b5[:, :width], -4.0, 4.0, ALU.max, ALU.min),
        deps=(T["r"],) + g(p1, "c1", "c3"))
    T["b2"] = sched.op("dve", lambda e: e.tensor_scalar(
        b3[:, :width], b5[:, :width], -2.0, 2.0, ALU.max, ALU.min),
        deps=(T["b1"],) + g(p1, "c1", "c2", "g"))
    T["s1"] = sched.op("act", lambda e: e.activation(
        s1v[:, :width], uf, AF.Sign, bias=bufs["neg5"][:, 0:1]),
        deps=(T["u"], T["b2"]) + g(p1, "c2"))
    T["s2"] = sched.op("act", lambda e: e.activation(
        s2v[:, :width], uf, AF.Sign, bias=bufs["pos5"][:, 0:1]),
        deps=(T["s1"],))
    def emit_tail():
        T["c1"] = sched.op(ceng, lambda e: e.tensor_tensor(
            b2[:, :width], b2[:, :width], b3[:, :width], ALU.subtract),
            deps=(T["b2"],))
        T["c2"] = sched.op(ceng, lambda e: e.tensor_tensor(
            b4[:, :width], s1v[:, :width], s2v[:, :width], ALU.add),
            deps=(T["s2"], T["c1"]) + g(p1, "g"))
        T["c3"] = sched.op("dve", lambda e: e.tensor_tensor(
            b1[:, :width], b1[:, :width], b2[:, :width], ALU.add),
            deps=(T["c1"], T["a2"]))
        T["g"] = sched.op("dve", lambda e: e.tensor_tensor(
            b1[:, :width], b1[:, :width], b4[:, :width], ALU.add),
            deps=(T["c3"], T["c2"]))
        q3 = q_out_ap.rearrange("p (b k) -> p b k", k=16)
        g3 = b1[:, :width].rearrange("p (b k) -> p b k", k=16)
        T["q"] = sched.op(qeng, lambda e: e.tensor_tensor(q3, g3, s_b, ALU.mult),
                          deps=(T["g"],) + tuple(extra_q_deps))
        return T
    if defer_tail:
        T["_tail"] = emit_tail
        return T
    emit_tail()
    return T


def make_quant_bufs(ar, base, width, tag):
    cur = base
    out = {}
    for nm, dt_ in (("u", F32), ("b5", F32)):
        out[nm] = ar.at(f"{tag}_{nm}", cur, [128, width], dt_)
        cur += width * 4
    for nm in ("b1", "b2", "b3", "b4", "s1v", "s2v"):
        out[nm] = ar.at(f"{tag}_{nm}", cur, [128, width], BF16)
        cur += width * 2
    for nm in ("amax", "s", "rs"):
        out[nm] = ar.at(f"{tag}_{nm}", cur, [128, width // 16], F32)
        cur += max(32, width // 16 * 4)
        cur = (cur + 31) // 32 * 32
    return out, cur


def build_launch_a():
    nc = bass.Bass("TRN2", target_bir_lowering=False, debug=False)
    WG_R = I // NCORES
    WD_R = D // NCORES
    wg_d = nc.dram_tensor("wg", [WG_R, D], F32, kind="ExternalInput").ap()
    wu_d = nc.dram_tensor("wu", [WG_R, D], F32, kind="ExternalInput").ap()
    wd_d = nc.dram_tensor("wd", [WD_R, I], F32, kind="ExternalInput").ap()
    wgq_d = nc.dram_tensor("wgq", [WG_R, D], BF16, kind="ExternalOutput").ap()
    wuq_d = nc.dram_tensor("wuq", [WG_R, D], BF16, kind="ExternalOutput").ap()
    wdq_d = nc.dram_tensor("wdq", [WD_R, I], BF16, kind="ExternalOutput").ap()

    es = ExitStack()
    sched = Sched(nc)
    sched.make_sems(es)
    W = 3072
    ar = Arena(nc, 200 * 1024)
    ar.reserve(es)
    cur = 0

    def alloc(name, shape, dt_):
        nonlocal cur
        h = ar.at(name, cur, shape, dt_)
        cur += (int(np.prod(shape[1:])) * mybir.dt.size(dt_) + 31) // 32 * 32
        return h
    src = [alloc("srcA", [128, W], F32), alloc("srcB", [128, W], F32)]
    qout = [alloc("qoutA", [128, W], BF16), alloc("qoutB", [128, W], BF16)]
    bsets = []
    for p in range(2):
        bufs = {}
        for nm, dt_ in (("u", F32), ("b5", F32)):
            bufs[nm] = alloc(f"q{p}_{nm}", [128, W], dt_)
        for nm in ("b1", "b2", "b3", "b4", "s1v", "s2v"):
            bufs[nm] = alloc(f"q{p}_{nm}", [128, W], BF16)
        for nm in ("amax", "s", "rs"):
            bufs[nm] = alloc(f"q{p}_{nm}", [128, W // 16], F32)
        bsets.append(bufs)
    neg5 = alloc("neg5", [128, 1], F32)
    pos5 = alloc("pos5", [128, 1], F32)
    for p in range(2):
        bsets[p]["neg5"] = neg5
        bsets[p]["pos5"] = pos5
    ar.finalize()

    t0 = [sched.op("dve", lambda e, p=p: e.memset(src[p][:], 0.0))
          for p in range(2)]
    t_n5 = sched.op("dve", lambda e: e.memset(neg5[:], -5.0))
    t_p5 = sched.op("dve", lambda e: e.memset(pos5[:], 5.0))

    jobs = []
    for w_in, w_out in ((wg_d, wgq_d), (wu_d, wuq_d)):
        loads = [(w_in[0:128, :], (0, 128, 0)),
                 (w_in[128:256, :], (0, 128, 1024)),
                 (w_in[256:352, :], (0, 96, 2048))]
        stores = [(w_out[0:128, :], (0, 128, 0)),
                  (w_out[128:256, :], (0, 128, 1024)),
                  (w_out[256:352, :], (0, 96, 2048))]
        jobs.append((loads, stores, 3072))
    jobs.append(([(wd_d[:], (0, 128, 0))], [(wdq_d[:], (0, 128, 0))], 2816))

    last = {0: (t0[0], t_n5, t_p5), 1: (t0[1],)}
    prevq = {0: None, 1: None}
    for ji, (loads, stores, width) in enumerate(jobs):
        p = ji % 2
        t_lds = []
        for iap, (r0, r1, c0) in loads:
            t_lds.append(sched.dma("sp", src[p][r0:r1, c0:c0 + 1024
                                               if width == 3072 else c0 + 2816],
                                   iap, deps=last[p]))
        T = emit_quant(sched, bsets[p], src[p][:, 0:width],
                       qout[p][:, 0:width], width, tuple(t_lds),
                       prev1=prevq[p], prev2=prevq[p])
        t_q = T["q"]
        t_sts = []
        for oap, (r0, r1, c0) in stores:
            t_sts.append(sched.dma("sp", oap,
                                   qout[p][r0:r1, c0:c0 + 1024
                                           if width == 3072 else c0 + 2816],
                                   deps=(t_q,)))
        last[p] = tuple(t_sts) + (t_q,)
        prevq[p] = T

    finals = sched.all_dma_barrier("sp")
    sched.finalize()
    with es, nc.Block() as block:
        @block.sync
        def _(sync):
            sched.replay("sp", sync)
            for (sname, v) in finals:
                sync.wait_ge(sched.sems[sname], v)

        @block.vector
        def _(v):
            sched.replay("dve", v)

        @block.scalar
        def _(s):
            sched.replay("act", s)

        @block.gpsimd
        def _(g):
            sched.replay("pool", g)

        @block.tensor
        def _(t):
            sched.replay("pe", t)
    return nc


def build_launch_b():
    nc = bass.Bass("TRN2", target_bir_lowering=False, debug=False)
    x_d = nc.dram_tensor("x", [N, D], F32, kind="ExternalInput").ap()
    w1_d = nc.dram_tensor("w1", [D, D], F32, kind="ExternalInput").ap()
    w2_d = nc.dram_tensor("w2", [D, D], F32, kind="ExternalInput").ap()
    bias_d = nc.dram_tensor("bias", [1, D], F32, kind="ExternalInput").ap()
    et_d = nc.dram_tensor("et", [21, D], F32, kind="ExternalInput").ap()
    id_d = nc.dram_tensor("ident", [21, 21], F32, kind="ExternalInput").ap()
    eye_d = nc.dram_tensor("eye128", [128, 128], F32, kind="ExternalInput").ap()
    wgq_d = nc.dram_tensor("wgq", [I, D], BF16, kind="ExternalInput").ap()
    wuq_d = nc.dram_tensor("wuq", [I, D], BF16, kind="ExternalInput").ap()
    wdq_d = nc.dram_tensor("wdq", [D, I], BF16, kind="ExternalInput").ap()
    out_d = nc.dram_tensor("out", [N, D], F32, kind="ExternalOutput").ap()

    _ik = "ExternalOutput" if DEBUG_TAPS else "Internal"
    xb_dram = nc.dram_tensor("xb_dram", [N, D], FP16, kind="Internal").ap()
    y1_dram = nc.dram_tensor("y1_dram", [N, D], F32, kind=_ik).ap()
    y1q_dram = nc.dram_tensor("y1q_dram", [N, D], BF16, kind=_ik).ap()
    hq_dram = nc.dram_tensor("hq_dram", [N, I], BF16, kind=_ik).ap()

    es = ExitStack()
    sched = Sched(nc)
    sched.make_sems(es)

    AR = 207 * 1024
    ar = Arena(nc, AR)
    ar.reserve(es)

    y1qT = ar.at("y1qT", 0, [128, KC * N], BF16)
    convA = ar.at("convA", 0, [128, N], FP16)
    convB = ar.at("convB", 4096, [128, N], FP16)
    convC = ar.at("convC", 8192, [128, N], FP16)
    convD = ar.at("convD", 12288, [128, N], FP16)
    convE = ar.at("convE", 16384, [128, N], FP16)
    et_rows = ar.at("et_rows", 20480, [21, D], F32)
    ident21 = ar.at("ident21", 24576, [21, 21], F32)

    GEN = 32768
    x_res = ar.at("x_res", GEN, [128, NT * D], F32)
    xT = ar.at("xT", GEN + 65536, [128, KC * (N + PAD)], FP16)
    hT = ar.at("hT", GEN + 98816, [128, KC * N], FP16)
    W1b = ar.at("W1b", GEN + 131584, [128, KC * D], FP16)
    W2b = ar.at("W2b", GEN + 148096, [128, KC * D], FP16)

    qb15 = {
        "u": ar.at("q15_u", 197248, [128, QW], F32),
        "b5": ar.at("q15_b5", 201344, [128, QW], F32),
        "amax": ar.at("q15_amax", 205440, [128, QW // 16], F32),
        "s": ar.at("q15_s", 205728, [128, QW // 16], F32),
        "rs": ar.at("q15_rs", 206016, [128, QW // 16], F32),
        "b1": ar.at("q15_b1", 2048, [128, QW], BF16),
        "b2": ar.at("q15_b2", 6144, [128, QW], BF16),
        "b3": ar.at("q15_b3", 10240, [128, QW], BF16),
        "b4": ar.at("q15_b4", 14336, [128, QW], BF16),
        "s1v": ar.at("q15_s1v", 18432, [128, QW], BF16),
        "s2v": ar.at("q15_s2v", 22528, [128, QW], BF16),
    }
    y1q_st = [ar.at("y1qs0", 26624, [128, D], BF16),
              ar.at("y1qs1", 30720, [128, D], BF16)]
    diag16 = ar.at("diag16", 197248, [128, len(PE_TAPS) * 128], FP16)
    amaxB = ar.at("amaxB", 197248, [128, I // 16], F32)
    sB = ar.at("sB", 197984, [128, I // 16], F32)
    rsB = ar.at("rsB", 198720, [128, I // 16], F32)

    wgT = ar.at("wgT", 32768, [128, KC * I], BF16)
    wuT = ar.at("wuT", 77824, [128, KC * I], BF16)
    qb2 = {
        "u": ar.at("q2_u", 122880, [128, I], F32),
        "b5": ar.at("q2_b5", 134144, [128, I], F32),
        "s1v": ar.at("q2_s1v", 134144, [128, I], BF16),
        "s2v": ar.at("q2_s2v", 139776, [128, I], BF16),
        "b1": ar.at("q2_b1", 145408, [128, I], BF16),
        "b2": ar.at("q2_b2", 151040, [128, I], BF16),
        "b3": ar.at("q2_b3", 156672, [128, I], BF16),
        "b4": ar.at("q2_b4", 156672, [128, I], BF16),
        "amax": ar.at("q2_amax", 162304, [128, I // 16], F32),
        "s": ar.at("q2_s", 163040, [128, I // 16], F32),
        "rs": ar.at("q2_rs", 163776, [128, I // 16], F32),
    }
    h_t = ar.at("h_t", 164512, [128, I], F32)
    h_tB = ar.at("h_tB", 175776, [128, I], F32)
    hq_t = ar.at("hq_t", 187040, [128, I], BF16)
    sg_b = [None] * 4
    up_b = [None] * 4
    cur2 = 199456
    for i in range(4):
        sg_b[i] = ar.at(f"sg{i}", cur2, [128, 512], BF16); cur2 += 1024
        up_b[i] = ar.at(f"up{i}", cur2, [128, 512], BF16); cur2 += 1024
    assert cur2 <= 208160, cur2

    hqT = ar.at("hqT", 32768, [128, IC * N], BF16)
    wdT = ar.at("wdT", 122880, [128, IC * D], BF16)
    y1r = ar.at("y1r", 167936, [128, 3 * D], F32)
    outb = ar.at("outb", 180224, [128, 3 * D], F32)

    sm = 208160
    et_fm = ar.at("et_fm", sm, [128, KC * 21], F32); sm += KC * 21 * 4
    sm = (sm + 31) // 32 * 32
    biasb = ar.at("biasb", sm, [1, D], BF16); sm += D * 2
    onesb = ar.at("onesb", sm, [1, 128], BF16); sm += 256
    neg5 = ar.at("neg5", sm, [128, 1], F32); sm += 32
    pos5 = ar.at("pos5", sm, [128, 1], F32); sm += 32
    epsc = ar.at("epsc", sm, [128, 1], F32); sm += 32
    mvst = ar.at("mvst", sm, [128, 8], F32); sm += 32
    bnbuf = ar.at("bnbuf", sm, [128, 12], F32); sm += 64
    rstdc = ar.at("rstdc", sm, [128, 1], F32); sm += 32
    sqc = ar.at("sqc", sm, [128, 1], F32); sm += 32
    nmr = ar.at("nmr", sm, [128, 1], F32); sm += 32
    eye16 = ar.at("eye16", sm, [128, 128], FP16); sm += 256
    assert sm <= AR, sm

    att_ps = [es.enter_context(nc.psum_tensor(f"aps{i}", [128, 1024], F32))
              for i in range(4)]

    ar.finalize()

    t_n5 = sched.op("dve", lambda e: e.memset(neg5[:], -5.0))
    t_p5 = sched.op("dve", lambda e: e.memset(pos5[:], 5.0))
    t_ep = sched.op("dve", lambda e: e.memset(epsc[:], 1e-5))
    t_ones = sched.op("dve", lambda e: e.memset(onesb[:], 1.0))
    t_pad = [sched.op("dve", lambda e, k=k: e.memset(
        xT[:, k * (N + PAD): k * (N + PAD) + PAD], 0.0)) for k in range(KC)]

    t_xres = [sched.dma("sp", x_res[:, t * D:(t + 1) * D],
                        x_d[t * 128:(t + 1) * 128, :]) for t in range(NT)]
    t_xb = sched.dma("pool", xb_dram[:], x_d[:])
    t_xT = [sched.dma("sp", xT[:, k * (N + PAD) + PAD:(k + 1) * (N + PAD)],
                      xb_dram[:, k * 128:(k + 1) * 128],
                      deps=(t_xb, t_pad[k]), transpose=True) for k in range(KC)]
    t_w1 = [sched.dma("pool", W1b[:, k * D:(k + 1) * D],
                      w1_d[k * 128:(k + 1) * 128, :]) for k in range(KC)]
    t_w2 = [sched.dma("pool", W2b[:, k * D:(k + 1) * D],
                      w2_d[k * 128:(k + 1) * 128, :]) for k in range(KC)]
    t_bias = sched.dma("pool", biasb[:], bias_d[:])
    t_eye = sched.dma("pool", eye16[:], eye_d[:])
    t_etl = sched.dma("sp", et_rows[:], et_d[:])
    t_id = [sched.dma("sp", ident21[:], id_d[:])]
    t_etfm = []
    for k in range(KC):
        t_tr = sched.op("pe", lambda e, k=k: e.transpose(
            att_ps[0][:, 0:21], et_rows[0:21, k * 128:(k + 1) * 128], ident21[:]),
            deps=tuple(t_id) + (t_etl, t_etfm[-1] if t_etfm else None))
        t_cp = sched.op("dve", lambda e, k=k: e.tensor_copy(
            et_fm[:, k * 21:(k + 1) * 21], att_ps[0][:, 0:21]), deps=(t_tr,))
        t_etfm.append(t_cp)
    t_et_done = t_etfm[-1]

    def xsh(k, j):
        base = k * (N + PAD) + PAD - j
        return xT[:, base: base + N]

    def etc(k, j):
        return et_fm[:, k * 21 + j: k * 21 + j + 1]

    conv_done = []
    conv_ps = [att_ps[3][:, 0:512], att_ps[3][:, 512:1024]]
    cp_guard = [None, None]
    cpar = [0]
    for k in range(KC):
        deps0 = (t_xT[k], t_et_done)
        npe = len(PE_TAPS)
        eye_b = eye16[:].unsqueeze(1).broadcast_to([128, npe, 128])
        etv = et_fm[:, k * 21 + PE_TAPS[0]: k * 21 + PE_TAPS[0] + npe]
        et_b = etv.unsqueeze(2).broadcast_to([128, npe, 128])
        t_dg = sched.op("dve", lambda e, k=k, eye_b=eye_b, et_b=et_b: e.tensor_tensor(
            diag16[:].rearrange("p (j m) -> p j m", m=128), eye_b, et_b,
            ALU.mult), deps=(t_et_done, t_eye) +
            (tuple(conv_done[-1][1]) if conv_done else ()))
        pe_evacs = []
        for nt in range(4):
            par = cpar[0] % 2
            cpar[0] += 1

            def mk_conv(e, k=k, nt=nt, par=par):
                last = None
                for ji, j in enumerate(PE_TAPS):
                    e.ldweights(diag16[:, ji * 128:(ji + 1) * 128])
                    last = e.matmul(
                        conv_ps[par], diag16[:, ji * 128:(ji + 1) * 128],
                        xT[:, k * (N + PAD) + PAD - j + nt * 512:
                           k * (N + PAD) + PAD - j + nt * 512 + 512],
                        start=(ji == 0), stop=(ji == len(PE_TAPS) - 1))
                return last
            cdeps = (t_dg, t_xT[k])
            if cp_guard[par] is not None:
                cdeps = cdeps + (cp_guard[par],)
            t_cmm = sched.op("pe", mk_conv, deps=cdeps)
            t_evc = sched.op("act", lambda e, k=k, nt=nt, par=par: e.activation(
                hT[:, k * N + nt * 512: k * N + (nt + 1) * 512],
                conv_ps[par], AF.Identity), deps=(t_cmm,))
            cp_guard[par] = t_evc
            pe_evacs.append(t_evc)
        prev = None
        dve_buf = None
        for ji, j in enumerate(DVE_TAPS):
            dst = convA if ji % 2 == 0 else convB
            srcb = convB if ji % 2 == 0 else convA
            if ji == 0:
                prev = sched.op("dve", lambda e, k=k, j=j, dst=dst: e.tensor_scalar(
                    dst[:], xsh(k, j), etc(k, j), None, ALU.mult, ALU.bypass),
                    deps=deps0 + ((conv_done[-1][0],) if conv_done else ()))
            else:
                prev = sched.op("dve",
                                lambda e, k=k, j=j, dst=dst, srcb=srcb:
                                e.scalar_tensor_tensor(dst[:], xsh(k, j), etc(k, j),
                                                       srcb[:], ALU.mult, ALU.add),
                                deps=(prev,))
            dve_buf = dst
        dve_last = prev
        et_bc1 = etc(k, POOL_TAPS[0]).broadcast_to([128, N])
        et_bc2 = etc(k, POOL_TAPS[1]).broadcast_to([128, N])
        tp1 = sched.op("pool", lambda e, k=k, et_bc1=et_bc1: e.tensor_tensor(
            convC[:], xsh(k, POOL_TAPS[0]), et_bc1, ALU.mult),
            deps=deps0 + ((conv_done[-1][0],) if conv_done else ()))
        tp2 = sched.op("pool", lambda e, k=k, et_bc2=et_bc2: e.tensor_tensor(
            convD[:], xsh(k, POOL_TAPS[1]), et_bc2, ALU.mult), deps=(tp1,))
        tp3 = sched.op("pool", lambda e: e.tensor_tensor(
            convE[:], convC[:], convD[:], ALU.add), deps=(tp2,))
        t_h1 = sched.op("dve", lambda e, dve_buf=dve_buf: e.tensor_tensor(
            convD[:], dve_buf[:], convE[:], ALU.add), deps=(dve_last, tp3))
        t_h2 = sched.op("dve", lambda e, k=k: e.tensor_tensor(
            hT[:, k * N:(k + 1) * N], hT[:, k * N:(k + 1) * N], convD[:],
            ALU.add), deps=(t_h1,) + tuple(pe_evacs))
        conv_done.append((t_h2, pe_evacs))
    conv_done = [c[0] for c in conv_done]

    def mk_mm(e, t, ps):
        last = None
        for k in range(KC):
            stat = xT[:, k * (N + PAD) + PAD + t * 128:
                      k * (N + PAD) + PAD + (t + 1) * 128]
            e.ldweights(stat)
            e.matmul(ps[:, 0:512], stat, W1b[:, k * D: k * D + 512],
                     start=(k == 0), stop=False)
            last = e.matmul(ps[:, 512:1024], stat,
                            W1b[:, k * D + 512:(k + 1) * D],
                            start=(k == 0), stop=False)
        return last

    def mk_mm2(e, t, ps):
        for k in range(KC):
            stat = hT[:, k * N + t * 128: k * N + (t + 1) * 128]
            e.ldweights(stat)
            e.matmul(ps[:, 0:512], stat, W2b[:, k * D: k * D + 512],
                     start=False, stop=False)
            e.matmul(ps[:, 512:1024], stat, W2b[:, k * D + 512:(k + 1) * D],
                     start=False, stop=False)
        e.ldweights(onesb[0:1, :])
        e.matmul(ps[:, 0:512], onesb[0:1, :], biasb[0:1, 0:512],
                 start=False, stop=True)
        return e.matmul(ps[:, 512:1024], onesb[0:1, :], biasb[0:1, 512:1024],
                        start=False, stop=True)

    attn_evac = [None] * NT
    tok_x = [None] * NT
    ln_nm = [None] * NT
    t_y1d = [None] * NT
    ln_prev = None
    q15 = dict(qb15)
    q15["neg5"] = neg5
    q15["pos5"] = pos5
    y1q_tok = [None] * NT
    y1q_dump = [None] * NT
    qprev = {}
    prev_dump = {0: None, 1: None}

    for t in range(3):
        tok_x[t] = sched.op("pe", lambda e, t=t: mk_mm(e, t, att_ps[t % 3]),
                            deps=tuple(t_xT) + tuple(t_w1) + (t_et_done,))
    for t in range(NT):
        ps = att_ps[t % 3]
        tok_h = sched.op("pe", lambda e, t=t, ps=ps: mk_mm2(e, t, ps),
                         deps=tuple(conv_done) + tuple(t_w2) +
                         (tok_x[t], t_ones, t_bias))
        t_ev = sched.op("dve", lambda e, t=t, ps=ps: e.tensor_tensor(
            x_res[:, t * D:(t + 1) * D], ps[:, :], x_res[:, t * D:(t + 1) * D],
            ALU.add), deps=(tok_h, t_xres[t]))
        attn_evac[t] = t_ev
        if t + 3 < NT:
            tok_x[t + 3] = sched.op(
                "pe", lambda e, t=t: mk_mm(e, t + 3, att_ps[(t + 3) % 3]),
                deps=(t_ev,))
        t_bn0 = sched.op("dve", lambda e, t=t: e.bn_stats(
            bnbuf[:, 0:6], x_res[:, t * D: t * D + 512]), deps=(t_ev, ln_prev))
        t_bn = sched.op("dve", lambda e, t=t: e.bn_stats(
            bnbuf[:, 6:12], x_res[:, t * D + 512:(t + 1) * D]), deps=(t_bn0,))
        t_ag = sched.op("dve", lambda e: e.bn_aggr(mvst[:, 0:2], bnbuf[:]),
                        deps=(t_bn,))
        t_sq = sched.op("act", lambda e: e.activation(
            sqc[:], mvst[:, 1:2], AF.Sqrt, bias=epsc[:, 0:1]), deps=(t_ag,))
        t_rsd = sched.op("dve", lambda e: e.reciprocal(rstdc[:], sqc[:]),
                         deps=(t_sq,))
        t_nm0 = sched.op("dve", lambda e: e.tensor_scalar(
            nmr[:], mvst[:, 0:1], rstdc[:, 0:1], -1.0, ALU.mult, ALU.mult),
            deps=(t_rsd,))
        t_nm = sched.op("act", lambda e, t=t: e.activation(
            x_res[:, t * D:(t + 1) * D], x_res[:, t * D:(t + 1) * D],
            AF.Identity, bias=nmr[:, 0:1], scale=rstdc[:, 0:1]),
            deps=(t_nm0, t_sq))
        ln_nm[t] = t_nm
        t_y1d[t] = sched.dma("sp", y1_dram[t * 128:(t + 1) * 128, :],
                             x_res[:, t * D:(t + 1) * D], deps=(t_nm,))
        ln_prev = t_nm
        stg = y1q_st[t % 2]
        qdeps = ((prev_dump[t % 2],) if prev_dump[t % 2] else ())
        Tq = emit_quant(sched, q15, x_res[:, t * D:(t + 1) * D],
                        stg[:], QW, (t_nm,), extra_q_deps=qdeps,
                        pool_combines=True,
                        prev1=qprev.get(t - 1), prev2=qprev.get(t - 1))
        qprev[t] = Tq
        y1q_tok[t] = Tq["q"]
        y1q_dump[t] = sched.dma("sp", y1q_dram[t * 128:(t + 1) * 128, :],
                                stg[:], deps=(Tq["q"],))
        prev_dump[t % 2] = y1q_dump[t]

    y1qT_half = [None, None]
    for half in range(2):
        y1qT_half[half] = [
            sched.dma("sp", y1qT[:, k * N + half * 1024: k * N + half * 1024 + 1024],
                      y1q_dram[half * 1024:(half + 1) * 1024,
                               k * 128:(k + 1) * 128],
                      deps=tuple(y1q_dump[half * 8:(half + 1) * 8]),
                      transpose=True) for k in range(KC)]

    t_wgT = [sched.dma("sp", wgT[:, k * I:(k + 1) * I],
                       wgq_d[:, k * 128:(k + 1) * 128], transpose=True,
                       deps=(y1q_tok[-1],) + tuple(t_y1d)) for k in range(KC)]
    t_wuT = [sched.dma("sp", wuT[:, k * I:(k + 1) * I],
                       wuq_d[:, k * 128:(k + 1) * 128], transpose=True,
                       deps=(y1q_tok[-1],) + tuple(t_y1d)) for k in range(KC)]


    IBANKS = [(ib * 512, min(512, I - ib * 512)) for ib in range((I + 511) // 512)]
    hq2 = dict(qb2)
    hq2["neg5"] = neg5
    hq2["pos5"] = pos5
    t_hq_store = [None] * NT
    psum_guard = {}
    h_guard = {0: None, 1: None}
    prev_st = [None]
    hqprev = {}
    hbufs = [h_t, h_tB]
    h_parts_of = [None] * NT
    pslot = [0]

    def emit_ffn_tile(t):
        hb = hbufs[t % 2]
        h_parts = []
        for bi, (i0, w) in enumerate(IBANKS):
            ps = att_ps[pslot[0] % 4]
            sg = sg_b[pslot[0] % 4]
            uv = up_b[pslot[0] % 4]
            deps = tuple(t_wgT) + tuple(t_wuT) + tuple(y1qT_half[t // 8])
            if psum_guard.get(pslot[0] % 4) is not None:
                deps = deps + (psum_guard[pslot[0] % 4],)
            pslot[0] += 1

            def mk_gmm(e, t=t, i0=i0, w=w, ps=ps):
                last = None
                for k in range(KC):
                    stat = y1qT[:, k * N + t * 128: k * N + (t + 1) * 128]
                    e.ldweights(stat)
                    e.matmul(ps[:, 0:w], stat, wgT[:, k * I + i0: k * I + i0 + w],
                             start=(k == 0), stop=(k == KC - 1))
                    last = e.matmul(ps[:, 512:512 + w], stat,
                                    wuT[:, k * I + i0: k * I + i0 + w],
                                    start=(k == 0), stop=(k == KC - 1))
                return last
            t_mm = sched.op("pe", mk_gmm, deps=deps)
            t_sg = sched.op("act", lambda e, sg=sg, ps=ps, w=w: e.activation(
                sg[:, 0:w], ps[:, 0:w], AF.Silu), deps=(t_mm,))
            t_uv = sched.op("act", lambda e, uv=uv, ps=ps, w=w: e.activation(
                uv[:, 0:w], ps[:, 512:512 + w], AF.Identity), deps=(t_sg,))
            heng = "pool"
            t_hm = sched.op(heng,
                            lambda e, hb=hb, sg=sg, uv=uv, i0=i0, w=w:
                            e.tensor_tensor(hb[:, i0:i0 + w], sg[:, 0:w],
                                            uv[:, 0:w], ALU.mult),
                            deps=(t_uv, h_guard[t % 2]))
            psum_guard[(pslot[0] - 1) % 4] = t_hm
            h_parts.append(t_hm)
        h_parts_of[t] = h_parts

    def emit_quant_head(t):
        hb = hbufs[t % 2]
        bset = dict(hq2)
        if t % 2 == 1:
            bset["amax"], bset["s"], bset["rs"] = amaxB, sB, rsB
        Tq = emit_quant(sched, bset, hb[:], hq_t[:], I,
                        tuple(h_parts_of[t]),
                        extra_q_deps=((prev_st[0],) if prev_st[0] else ()),
                        pool_combines=True, pool_q=False,
                        prev1=hqprev.get(t - 1), prev2=hqprev.get(t - 2),
                        defer_tail=True)
        hqprev[t] = Tq
        h_guard[t % 2] = Tq["u"]
        return Tq

    def emit_quant_tail(t, Tq):
        Tq["_tail"]()
        t_st = sched.dma("sp", hq_dram[t * 128:(t + 1) * 128, :],
                         hq_t[0:128, :], deps=(Tq["q"],))
        t_hq_store[t] = t_st
        prev_st[0] = t_st

    pending = [None]
    for t in range(NT):
        if t >= 1:
            pending[0] = (t - 1, emit_quant_head(t - 1))
        emit_ffn_tile(t)
        if pending[0] is not None:
            emit_quant_tail(*pending[0])
            pending[0] = None
    Tl = emit_quant_head(NT - 1)
    emit_quant_tail(NT - 1, Tl)

    t_wdT = [sched.dma("sp", wdT[:, ic * D:(ic + 1) * D],
                       wdq_d[:, ic * 128:(ic + 1) * 128], transpose=True,
                       deps=tuple(t_hq_store)) for ic in range(IC)]
    t_hqT = []
    for half in range(2):
        t_hqT.append([sched.dma(
            "sp", hqT[:, ic * N + half * 1024: ic * N + half * 1024 + 1024],
            hq_dram[half * 1024:(half + 1) * 1024, ic * 128:(ic + 1) * 128],
            transpose=True,
            deps=tuple(tk for tk in t_hq_store[half * 8:(half + 1) * 8] if tk))
            for ic in range(IC)])

    def mk_dmm(e, t, ps):
        last = None
        for ic in range(IC):
            stat = hqT[:, ic * N + t * 128: ic * N + (t + 1) * 128]
            e.ldweights(stat)
            e.matmul(ps[:, 0:512], stat, wdT[:, ic * D: ic * D + 512],
                     start=(ic == 0), stop=(ic == IC - 1))
            last = e.matmul(ps[:, 512:1024], stat,
                            wdT[:, ic * D + 512:(ic + 1) * D],
                            start=(ic == 0), stop=(ic == IC - 1))
        return last

    down_evac = [None] * NT
    t_ostores = [None] * NT
    ln2_prev = None
    for t in range(NT):
        ps = att_ps[t % 4]
        deps = tuple(t_hqT[t // 8]) + tuple(t_wdT)
        if t >= 4:
            deps = deps + (down_evac[t - 4],)
        else:
            deps = deps + (psum_guard.get(t % 4),)
        t_mm = sched.op("pe", lambda e, t=t, ps=ps: mk_dmm(e, t, ps), deps=deps)
        yslot = y1r[:, (t % 3) * D:(t % 3 + 1) * D]
        ydeps = (t_hq_store[NT - 1],)
        if t >= 3:
            ydeps = ydeps + (down_evac[t - 3],)
        t_yl = sched.dma("sp", yslot, y1_dram[t * 128:(t + 1) * 128, :],
                         deps=ydeps)
        oslot = outb[:, (t % 3) * D:(t % 3 + 1) * D]
        odeps = (t_mm, t_yl, t_hq_store[NT - 1])
        if t >= 3:
            odeps = odeps + (t_ostores[t - 3],)
        t_ev = sched.op("dve",
                        lambda e, ps=ps, yslot=yslot, oslot=oslot: e.tensor_tensor(
                            oslot, ps[:, :], yslot, ALU.add), deps=odeps)
        down_evac[t] = t_ev
        t_bn0 = sched.op("dve", lambda e, oslot=oslot: e.bn_stats(
            bnbuf[:, 0:6], oslot[:, 0:512]), deps=(t_ev, ln2_prev))
        t_bn = sched.op("dve", lambda e, oslot=oslot: e.bn_stats(
            bnbuf[:, 6:12], oslot[:, 512:1024]), deps=(t_bn0,))
        t_ag = sched.op("dve", lambda e: e.bn_aggr(mvst[:, 2:4], bnbuf[:]),
                        deps=(t_bn,))
        t_sq = sched.op("act", lambda e: e.activation(
            sqc[:], mvst[:, 3:4], AF.Sqrt, bias=epsc[:, 0:1]), deps=(t_ag,))
        t_rsd = sched.op("dve", lambda e: e.reciprocal(rstdc[:], sqc[:]),
                         deps=(t_sq,))
        t_nm0 = sched.op("dve", lambda e: e.tensor_scalar(
            nmr[:], mvst[:, 2:3], rstdc[:, 0:1], -1.0, ALU.mult, ALU.mult),
            deps=(t_rsd,))
        t_nm = sched.op("act", lambda e, oslot=oslot: e.activation(
            oslot, oslot, AF.Identity, bias=nmr[:, 0:1], scale=rstdc[:, 0:1]),
            deps=(t_nm0, t_sq))
        t_ostores[t] = sched.dma("sp", out_d[t * 128:(t + 1) * 128, :],
                                 oslot, deps=(t_nm,))
        ln2_prev = t_nm

    finals = sched.all_dma_barrier("sp")
    sched.finalize()
    with es, nc.Block() as block:
        @block.sync
        def _(sync):
            sched.replay("sp", sync)
            for (sname, v) in finals:
                sync.wait_ge(sched.sems[sname], v)

        @block.vector
        def _(v):
            sched.replay("dve", v)

        @block.scalar
        def _(s):
            sched.replay("act", s)

        @block.gpsimd
        def _(g):
            sched.replay("pool", g)

        @block.tensor
        def _(t):
            sched.replay("pe", t)
    return nc


_NC_A = None
_NC_B = None


def kernel(L0_attn_input_data, L0_attn_W1, L0_attn_W2, L0_attn_bias,
           L0_attn_embeddings_table, L0_ffn_w_gate, L0_ffn_w_up,
           L0_ffn_w_down):
    global _NC_A, _NC_B
    x = np.ascontiguousarray(np.asarray(L0_attn_input_data, dtype=np.float32))
    W1 = np.ascontiguousarray(np.asarray(L0_attn_W1, dtype=np.float32))
    W2 = np.ascontiguousarray(np.asarray(L0_attn_W2, dtype=np.float32))
    bias = np.ascontiguousarray(
        np.asarray(L0_attn_bias, dtype=np.float32)).reshape(1, D)
    et = np.ascontiguousarray(
        np.asarray(L0_attn_embeddings_table, dtype=np.float32))
    wg = np.ascontiguousarray(np.asarray(L0_ffn_w_gate, dtype=np.float32))
    wu = np.ascontiguousarray(np.asarray(L0_ffn_w_up, dtype=np.float32))
    wd = np.ascontiguousarray(np.asarray(L0_ffn_w_down, dtype=np.float32))

    if _NC_A is None:
        _NC_A = build_launch_a()
    WG_R, WD_R = I // NCORES, D // NCORES
    in_a = [{"wg": wg[c * WG_R:(c + 1) * WG_R],
             "wu": wu[c * WG_R:(c + 1) * WG_R],
             "wd": wd[c * WD_R:(c + 1) * WD_R]} for c in range(NCORES)]
    res_a = run_bass_kernel_spmd(_NC_A, in_a, core_ids=list(range(NCORES)))
    wgq = np.concatenate([res_a.results[c]["wgq"] for c in range(NCORES)], axis=0)
    wuq = np.concatenate([res_a.results[c]["wuq"] for c in range(NCORES)], axis=0)
    wdq = np.concatenate([res_a.results[c]["wdq"] for c in range(NCORES)], axis=0)

    if _NC_B is None:
        _NC_B = build_launch_b()
    ident = np.eye(21, dtype=np.float32)
    eye128 = np.eye(128, dtype=np.float32)
    in_b = [{"x": x[c], "w1": W1, "w2": W2, "bias": bias, "et": et,
             "ident": ident, "eye128": eye128,
             "wgq": wgq, "wuq": wuq, "wdq": wdq} for c in range(NCORES)]
    res_b = run_bass_kernel_spmd(_NC_B, in_b, core_ids=list(range(NCORES)))
    out = np.stack([res_b.results[c]["out"] for c in range(NCORES)], axis=0)
    return out.astype(np.float32)


# version 10
# speedup vs baseline: 1.1320x; 1.1320x over previous
import numpy as np
import concourse.bass as bass
import concourse.mybir as mybir
from concourse.bass_utils import run_bass_kernel_spmd
from contextlib import ExitStack

F32 = mybir.dt.float32
BF16 = mybir.dt.bfloat16
FP16 = mybir.dt.float16
ALU = mybir.AluOpType
AF = mybir.ActivationFunctionType
AX = mybir.AxisListType

NCORES = 8
N, D, I = 2048, 1024, 2816
NT = N // 128
KC = D // 128
IC = I // 128
RND_C = 12582912.0
RND_HC = 6291456.0
PE_TAPS = list(range(1, 15))
DVE_TAPS = list(range(15, 19))
POOL_TAPS = list(range(19, 21))
PAD = 32
QW = 1024
DEBUG_TAPS = False


class Sched:

    ENGS = ("pe", "dve", "act", "pool", "sp")
    NPOOL = {"act": 12, "pool": 8, "sp": 20}

    def __init__(self, nc):
        self.nc = nc
        self.items = {e: [] for e in self.ENGS}
        self.count = {}
        self.sems = {}
        self.rot = {e: 0 for e in self.NPOOL}

    def make_sems(self, es):
        for e in self.ENGS:
            self.sems[e] = es.enter_context(self.nc.semaphore(f"sem_{e}"))
            self.count[e] = 0
        for e, np_ in self.NPOOL.items():
            for i in range(np_):
                q = f"{e}q{i}"
                self.sems[q] = es.enter_context(self.nc.semaphore(f"sem_{q}"))
                self.count[q] = 0

    def op(self, eng, fn, deps=()):
        self.count[eng] += 1
        tok = (eng, self.count[eng])
        self.items[eng].append(
            (fn, tuple(d for d in deps if d is not None), tok, 1, eng))
        return tok

    def dma(self, eng, out, in_, deps=(), **kw):
        assert eng in self.NPOOL
        q = f"{eng}q{self.rot[eng]}"
        self.rot[eng] = (self.rot[eng] + 1) % self.NPOOL[eng]
        prev = self.count[q]
        deps = tuple(d for d in deps if d is not None)
        if prev > 0:
            deps = deps + ((q, prev),)
        self.count[q] += 16
        tok = (q, self.count[q])
        self.items[eng].append(
            (lambda e: e.dma_start(out=out, in_=in_, **kw), deps, tok, 16, q))
        return tok

    def finalize(self):
        self.prior = {}
        for eng in self.ENGS:
            run = {}
            for (fn, deps, tok, inc, sem) in self.items[eng]:
                self.prior[tok] = dict(run)
                run[sem] = tok[1]

    def replay(self, eng, e):
        waited = {}
        for (fn, deps, tok, inc, sem) in self.items[eng]:
            want = {}
            for (s, v) in deps:
                if s == eng:
                    v = min(v, self.prior[tok].get(s, 0))
                if v > want.get(s, 0):
                    want[s] = v
            for s, v in want.items():
                if v > waited.get(s, 0):
                    e.wait_ge(self.sems[s], v)
                    waited[s] = v
            inst = fn(e)
            if inst is not None:
                inst.then_inc(self.sems[sem], inc)

    def barrier_token(self, eng, dma=False):
        return (eng, self.count[eng])

    def all_dma_barrier(self, eng):
        return tuple((f"{eng}q{i}", self.count[f"{eng}q{i}"])
                     for i in range(self.NPOOL[eng])
                     if self.count[f"{eng}q{i}"] > 0)


class Arena:
    def __init__(self, nc, total_bytes):
        self.nc = nc
        self.total = total_bytes
        self.handle = None
        self.named = []

    def reserve(self, es):
        self.handle = es.enter_context(
            self.nc.sbuf_tensor("arena", [128, self.total], mybir.dt.uint8))

    def at(self, name, offset, shape, dtype):
        per_part = int(np.prod(shape[1:])) * mybir.dt.size(dtype)
        assert offset % 32 == 0, (name, offset)
        assert offset + per_part <= self.total, (name, offset, per_part)
        h = self.nc.virtual_sbuf_tensor(name, shape, dtype)
        self.named.append((h, offset))
        return h

    def finalize(self):
        base = self.nc.lookup_mloc(self.handle).addr
        for h, off in self.named:
            self.nc._tensor(h.name, h.shape, h.dtype, type="SB")
            mloc = self.nc.lookup_mloc(h)
            mloc.allocated = not self.nc.target_bir_lowering
            mloc.addr = base + off


def emit_quant(sched, bufs, src_ap, q_out_ap, width, src_tok, extra_q_deps=(),
               pool_combines=False, pool_q=None, prev1=None, prev2=None,
               defer_tail=False, pool_u=False):
    nb = width // 16
    amax, s_t, rs_t = bufs["amax"], bufs["s"], bufs["rs"]
    u, b5 = bufs["u"], bufs["b5"]
    b1, b2, b3, b4 = bufs["b1"], bufs["b2"], bufs["b3"], bufs["b4"]
    s1v, s2v = bufs["s1v"], bufs["s2v"]
    ceng = "pool" if pool_combines else "dve"
    qeng = ceng if pool_q is None else ("pool" if pool_q else "dve")
    p1 = prev1 or {}
    p2 = prev2 or {}

    def g(d, *keys):
        return tuple(d[k] for k in keys if d.get(k) is not None)

    src_deps = tuple(src_tok) if isinstance(src_tok, (tuple, list)) and \
        src_tok and isinstance(src_tok[0], tuple) else (src_tok,)
    T = {}
    src3 = src_ap.rearrange("p (b k) -> p b k", k=16)
    T["am"] = sched.op("dve", lambda e: e.tensor_reduce(
        amax[:, :nb], src3, AX.X, ALU.max, apply_absolute_value=True),
        deps=src_deps + g(p2, "s"))
    T["s"] = sched.op("dve", lambda e: e.tensor_scalar(
        s_t[:, :nb], amax[:, :nb], 1.0 / 6.0, 1e-12, ALU.mult, ALU.max),
        deps=(T["am"],) + g(p2, "q"))
    T["rs"] = sched.op("dve", lambda e: e.reciprocal(rs_t[:, :nb], s_t[:, :nb]),
                       deps=(T["s"],) + g(p2, "u"))
    rs_b = rs_t[:, :nb].unsqueeze(2).broadcast_to([128, nb, 16])
    s_b = s_t[:, :nb].unsqueeze(2).broadcast_to([128, nb, 16])
    u3 = u[:, :width].rearrange("p (b k) -> p b k", k=16)
    T["u"] = sched.op("pool" if pool_u else "dve",
                      lambda e: e.tensor_tensor(u3, src3, rs_b, ALU.mult),
                      deps=(T["rs"],) + g(p1, "r", "s2"))
    uf = u[:, :width]
    T["rh"] = sched.op("dve", lambda e: e.tensor_scalar(
        b5[:, :width], uf, RND_HC, RND_HC, ALU.add, ALU.subtract),
        deps=(T["u"],) + g(p1, "a2", "c2"))
    T["a2"] = sched.op("dve", lambda e: e.tensor_scalar(
        b1[:, :width], b5[:, :width], -2.0, 2.0, ALU.max, ALU.min),
        deps=(T["rh"],) + g(p1, "g", "q"))
    T["r"] = sched.op("dve", lambda e: e.tensor_scalar(
        b5[:, :width], uf, RND_C, RND_C, ALU.add, ALU.subtract),
        deps=(T["a2"],))
    T["b1"] = sched.op("dve", lambda e: e.tensor_scalar(
        b2[:, :width], b5[:, :width], -4.0, 4.0, ALU.max, ALU.min),
        deps=(T["r"],) + g(p1, "c1", "c3"))
    T["b2"] = sched.op("dve", lambda e: e.tensor_scalar(
        b3[:, :width], b5[:, :width], -2.0, 2.0, ALU.max, ALU.min),
        deps=(T["b1"],) + g(p1, "c1", "c2", "g"))
    T["s1"] = sched.op("act", lambda e: e.activation(
        s1v[:, :width], uf, AF.Sign, bias=bufs["neg5"][:, 0:1]),
        deps=(T["u"], T["b2"]) + g(p1, "c2"))
    T["s2"] = sched.op("act", lambda e: e.activation(
        s2v[:, :width], uf, AF.Sign, bias=bufs["pos5"][:, 0:1]),
        deps=(T["s1"],))
    def emit_tail():
        T["c1"] = sched.op(ceng, lambda e: e.tensor_tensor(
            b2[:, :width], b2[:, :width], b3[:, :width], ALU.subtract),
            deps=(T["b2"],))
        T["c2"] = sched.op(ceng, lambda e: e.tensor_tensor(
            b4[:, :width], s1v[:, :width], s2v[:, :width], ALU.add),
            deps=(T["s2"], T["c1"]) + g(p1, "g"))
        T["c3"] = sched.op("dve", lambda e: e.tensor_tensor(
            b1[:, :width], b1[:, :width], b2[:, :width], ALU.add),
            deps=(T["c1"], T["a2"]))
        T["g"] = sched.op("dve", lambda e: e.tensor_tensor(
            b1[:, :width], b1[:, :width], b4[:, :width], ALU.add),
            deps=(T["c3"], T["c2"]))
        q3 = q_out_ap.rearrange("p (b k) -> p b k", k=16)
        g3 = b1[:, :width].rearrange("p (b k) -> p b k", k=16)
        T["q"] = sched.op(qeng, lambda e: e.tensor_tensor(q3, g3, s_b, ALU.mult),
                          deps=(T["g"],) + tuple(extra_q_deps))
        return T
    if defer_tail:
        T["_tail"] = emit_tail
        return T
    emit_tail()
    return T


def make_quant_bufs(ar, base, width, tag):
    cur = base
    out = {}
    for nm, dt_ in (("u", F32), ("b5", F32)):
        out[nm] = ar.at(f"{tag}_{nm}", cur, [128, width], dt_)
        cur += width * 4
    for nm in ("b1", "b2", "b3", "b4", "s1v", "s2v"):
        out[nm] = ar.at(f"{tag}_{nm}", cur, [128, width], BF16)
        cur += width * 2
    for nm in ("amax", "s", "rs"):
        out[nm] = ar.at(f"{tag}_{nm}", cur, [128, width // 16], F32)
        cur += max(32, width // 16 * 4)
        cur = (cur + 31) // 32 * 32
    return out, cur


def build_launch_a():
    nc = bass.Bass("TRN2", target_bir_lowering=False, debug=False)
    WG_R = I // NCORES
    WD_R = D // NCORES
    wg_d = nc.dram_tensor("wg", [WG_R, D], F32, kind="ExternalInput").ap()
    wu_d = nc.dram_tensor("wu", [WG_R, D], F32, kind="ExternalInput").ap()
    wd_d = nc.dram_tensor("wd", [WD_R, I], F32, kind="ExternalInput").ap()
    wgq_d = nc.dram_tensor("wgq", [WG_R, D], BF16, kind="ExternalOutput").ap()
    wuq_d = nc.dram_tensor("wuq", [WG_R, D], BF16, kind="ExternalOutput").ap()
    wdq_d = nc.dram_tensor("wdq", [WD_R, I], BF16, kind="ExternalOutput").ap()

    es = ExitStack()
    sched = Sched(nc)
    sched.make_sems(es)
    W = 3072
    ar = Arena(nc, 200 * 1024)
    ar.reserve(es)
    cur = 0

    def alloc(name, shape, dt_):
        nonlocal cur
        h = ar.at(name, cur, shape, dt_)
        cur += (int(np.prod(shape[1:])) * mybir.dt.size(dt_) + 31) // 32 * 32
        return h
    src = [alloc("srcA", [128, W], F32), alloc("srcB", [128, W], F32)]
    qout = [alloc("qoutA", [128, W], BF16), alloc("qoutB", [128, W], BF16)]
    bsets = []
    for p in range(2):
        bufs = {}
        for nm, dt_ in (("u", F32), ("b5", F32)):
            bufs[nm] = alloc(f"q{p}_{nm}", [128, W], dt_)
        for nm in ("b1", "b2", "b3", "b4", "s1v", "s2v"):
            bufs[nm] = alloc(f"q{p}_{nm}", [128, W], BF16)
        for nm in ("amax", "s", "rs"):
            bufs[nm] = alloc(f"q{p}_{nm}", [128, W // 16], F32)
        bsets.append(bufs)
    neg5 = alloc("neg5", [128, 1], F32)
    pos5 = alloc("pos5", [128, 1], F32)
    for p in range(2):
        bsets[p]["neg5"] = neg5
        bsets[p]["pos5"] = pos5
    ar.finalize()

    t0 = [sched.op("dve", lambda e, p=p: e.memset(src[p][:], 0.0))
          for p in range(2)]
    t_n5 = sched.op("dve", lambda e: e.memset(neg5[:], -5.0))
    t_p5 = sched.op("dve", lambda e: e.memset(pos5[:], 5.0))

    jobs = []
    for w_in, w_out in ((wg_d, wgq_d), (wu_d, wuq_d)):
        loads = [(w_in[0:128, :], (0, 128, 0)),
                 (w_in[128:256, :], (0, 128, 1024)),
                 (w_in[256:352, :], (0, 96, 2048))]
        stores = [(w_out[0:128, :], (0, 128, 0)),
                  (w_out[128:256, :], (0, 128, 1024)),
                  (w_out[256:352, :], (0, 96, 2048))]
        jobs.append((loads, stores, 3072))
    jobs.append(([(wd_d[:], (0, 128, 0))], [(wdq_d[:], (0, 128, 0))], 2816))

    last = {0: (t0[0], t_n5, t_p5), 1: (t0[1],)}
    prevq = {0: None, 1: None}
    for ji, (loads, stores, width) in enumerate(jobs):
        p = ji % 2
        t_lds = []
        for iap, (r0, r1, c0) in loads:
            t_lds.append(sched.dma("sp", src[p][r0:r1, c0:c0 + 1024
                                               if width == 3072 else c0 + 2816],
                                   iap, deps=last[p]))
        T = emit_quant(sched, bsets[p], src[p][:, 0:width],
                       qout[p][:, 0:width], width, tuple(t_lds),
                       prev1=prevq[p], prev2=prevq[p])
        t_q = T["q"]
        t_sts = []
        for oap, (r0, r1, c0) in stores:
            t_sts.append(sched.dma("sp", oap,
                                   qout[p][r0:r1, c0:c0 + 1024
                                           if width == 3072 else c0 + 2816],
                                   deps=(t_q,)))
        last[p] = tuple(t_sts) + (t_q,)
        prevq[p] = T

    finals = sched.all_dma_barrier("sp")
    sched.finalize()
    with es, nc.Block() as block:
        @block.sync
        def _(sync):
            sched.replay("sp", sync)
            for (sname, v) in finals:
                sync.wait_ge(sched.sems[sname], v)

        @block.vector
        def _(v):
            sched.replay("dve", v)

        @block.scalar
        def _(s):
            sched.replay("act", s)

        @block.gpsimd
        def _(g):
            sched.replay("pool", g)

        @block.tensor
        def _(t):
            sched.replay("pe", t)
    return nc


def build_launch_b():
    nc = bass.Bass("TRN2", target_bir_lowering=False, debug=False)
    x_d = nc.dram_tensor("x", [N, D], F32, kind="ExternalInput").ap()
    w1_d = nc.dram_tensor("w1", [D, D], F32, kind="ExternalInput").ap()
    w2_d = nc.dram_tensor("w2", [D, D], F32, kind="ExternalInput").ap()
    bias_d = nc.dram_tensor("bias", [1, D], F32, kind="ExternalInput").ap()
    et_d = nc.dram_tensor("et", [21, D], F32, kind="ExternalInput").ap()
    id_d = nc.dram_tensor("ident", [21, 21], F32, kind="ExternalInput").ap()
    eye_d = nc.dram_tensor("eye128", [128, 128], F32, kind="ExternalInput").ap()
    wgq_d = nc.dram_tensor("wgq", [I, D], BF16, kind="ExternalInput").ap()
    wuq_d = nc.dram_tensor("wuq", [I, D], BF16, kind="ExternalInput").ap()
    wdq_d = nc.dram_tensor("wdq", [D, I], BF16, kind="ExternalInput").ap()
    out_d = nc.dram_tensor("out", [N, D], F32, kind="ExternalOutput").ap()

    _ik = "ExternalOutput" if DEBUG_TAPS else "Internal"
    xb_dram = nc.dram_tensor("xb_dram", [N, D], FP16, kind="Internal").ap()
    y1_dram = nc.dram_tensor("y1_dram", [N, D], F32, kind=_ik).ap()
    y1q_dram = nc.dram_tensor("y1q_dram", [N, D], BF16, kind=_ik).ap()
    hq_dram = nc.dram_tensor("hq_dram", [N, I], BF16, kind=_ik).ap()

    es = ExitStack()
    sched = Sched(nc)
    sched.make_sems(es)

    AR = 207 * 1024
    ar = Arena(nc, AR)
    ar.reserve(es)

    y1qT = ar.at("y1qT", 0, [128, KC * N], BF16)
    convA = ar.at("convA", 0, [128, N], FP16)
    convB = ar.at("convB", 4096, [128, N], FP16)
    convC = ar.at("convC", 8192, [128, N], FP16)
    convD = ar.at("convD", 12288, [128, N], FP16)
    convE = ar.at("convE", 16384, [128, N], FP16)
    et_rows = ar.at("et_rows", 20480, [21, D], F32)
    ident21 = ar.at("ident21", 24576, [21, 21], F32)

    GEN = 32768
    x_res = ar.at("x_res", GEN, [128, NT * D], F32)
    xT = ar.at("xT", GEN + 65536, [128, KC * (N + PAD)], FP16)
    hT = ar.at("hT", GEN + 98816, [128, KC * N], FP16)
    W1b = ar.at("W1b", GEN + 131584, [128, KC * D], FP16)
    W2b = ar.at("W2b", GEN + 148096, [128, KC * D], FP16)

    qb15 = {
        "u": ar.at("q15_u", 197248, [128, QW], F32),
        "b5": ar.at("q15_b5", 201344, [128, QW], F32),
        "amax": ar.at("q15_amax", 205440, [128, QW // 16], F32),
        "s": ar.at("q15_s", 205728, [128, QW // 16], F32),
        "rs": ar.at("q15_rs", 206016, [128, QW // 16], F32),
        "b1": ar.at("q15_b1", 2048, [128, QW], BF16),
        "b2": ar.at("q15_b2", 6144, [128, QW], BF16),
        "b3": ar.at("q15_b3", 10240, [128, QW], BF16),
        "b4": ar.at("q15_b4", 14336, [128, QW], BF16),
        "s1v": ar.at("q15_s1v", 18432, [128, QW], BF16),
        "s2v": ar.at("q15_s2v", 22528, [128, QW], BF16),
    }
    y1q_st = [ar.at("y1qs0", 26624, [128, D], BF16),
              ar.at("y1qs1", 30720, [128, D], BF16)]
    diag16 = ar.at("diag16", 197248, [128, len(PE_TAPS) * 128], FP16)
    amaxB = ar.at("amaxB", 197248, [128, I // 16], F32)
    sB = ar.at("sB", 197984, [128, I // 16], F32)
    rsB = ar.at("rsB", 198720, [128, I // 16], F32)

    wgT = ar.at("wgT", 32768, [128, KC * I], BF16)
    wuT = ar.at("wuT", 77824, [128, KC * I], BF16)
    qb2 = {
        "u": ar.at("q2_u", 122880, [128, I], F32),
        "b5": ar.at("q2_b5", 134144, [128, I], F32),
        "s1v": ar.at("q2_s1v", 134144, [128, I], BF16),
        "s2v": ar.at("q2_s2v", 139776, [128, I], BF16),
        "b1": ar.at("q2_b1", 145408, [128, I], BF16),
        "b2": ar.at("q2_b2", 151040, [128, I], BF16),
        "b3": ar.at("q2_b3", 156672, [128, I], BF16),
        "b4": ar.at("q2_b4", 156672, [128, I], BF16),
        "amax": ar.at("q2_amax", 162304, [128, I // 16], F32),
        "s": ar.at("q2_s", 163040, [128, I // 16], F32),
        "rs": ar.at("q2_rs", 163776, [128, I // 16], F32),
    }
    h_t = ar.at("h_t", 164512, [128, I], F32)
    h_tB = ar.at("h_tB", 175776, [128, I], F32)
    hq_t = ar.at("hq_t", 187040, [128, I], BF16)
    sg_b = [None] * 4
    up_b = [None] * 4
    cur2 = 199456
    for i in range(4):
        sg_b[i] = ar.at(f"sg{i}", cur2, [128, 512], BF16); cur2 += 1024
        up_b[i] = ar.at(f"up{i}", cur2, [128, 512], BF16); cur2 += 1024
    assert cur2 <= 208160, cur2

    hqT = ar.at("hqT", 32768, [128, IC * N], BF16)
    wdT = ar.at("wdT", 122880, [128, IC * D], BF16)
    y1r = ar.at("y1r", 167936, [128, 3 * D], F32)
    outb = ar.at("outb", 180224, [128, 3 * D], F32)

    sm = 208160
    et_fm = ar.at("et_fm", sm, [128, KC * 21], F32); sm += KC * 21 * 4
    sm = (sm + 31) // 32 * 32
    biasb = ar.at("biasb", sm, [1, D], BF16); sm += D * 2
    onesb = ar.at("onesb", sm, [1, 128], BF16); sm += 256
    neg5 = ar.at("neg5", sm, [128, 1], F32); sm += 32
    pos5 = ar.at("pos5", sm, [128, 1], F32); sm += 32
    epsc = ar.at("epsc", sm, [128, 1], F32); sm += 32
    mvst = ar.at("mvst", sm, [128, 8], F32); sm += 32
    bnbuf = ar.at("bnbuf", sm, [128, 12], F32); sm += 64
    rstdc = ar.at("rstdc", sm, [128, 1], F32); sm += 32
    sqc = ar.at("sqc", sm, [128, 1], F32); sm += 32
    nmr = ar.at("nmr", sm, [128, 1], F32); sm += 32
    eye16 = ar.at("eye16", sm, [128, 128], FP16); sm += 256
    assert sm <= AR, sm

    att_ps = [es.enter_context(nc.psum_tensor(f"aps{i}", [128, 1024], F32))
              for i in range(4)]

    ar.finalize()

    t_n5 = sched.op("dve", lambda e: e.memset(neg5[:], -5.0))
    t_p5 = sched.op("dve", lambda e: e.memset(pos5[:], 5.0))
    t_ep = sched.op("dve", lambda e: e.memset(epsc[:], 1e-5))
    t_ones = sched.op("dve", lambda e: e.memset(onesb[:], 1.0))
    t_pad = [sched.op("dve", lambda e, k=k: e.memset(
        xT[:, k * (N + PAD): k * (N + PAD) + PAD], 0.0)) for k in range(KC)]

    t_xres = [sched.dma("sp", x_res[:, t * D:(t + 1) * D],
                        x_d[t * 128:(t + 1) * 128, :]) for t in range(NT)]
    t_xb = sched.dma("pool", xb_dram[:], x_d[:])
    t_xT = [sched.dma("sp", xT[:, k * (N + PAD) + PAD:(k + 1) * (N + PAD)],
                      xb_dram[:, k * 128:(k + 1) * 128],
                      deps=(t_xb, t_pad[k]), transpose=True) for k in range(KC)]
    t_w1 = [sched.dma("pool", W1b[:, k * D:(k + 1) * D],
                      w1_d[k * 128:(k + 1) * 128, :]) for k in range(KC)]
    t_w2 = [sched.dma("pool", W2b[:, k * D:(k + 1) * D],
                      w2_d[k * 128:(k + 1) * 128, :]) for k in range(KC)]
    t_bias = sched.dma("pool", biasb[:], bias_d[:])
    t_eye = sched.dma("pool", eye16[:], eye_d[:])
    t_etl = sched.dma("sp", et_rows[:], et_d[:])
    t_id = [sched.dma("sp", ident21[:], id_d[:])]
    t_etfm = []
    for k in range(KC):
        t_tr = sched.op("pe", lambda e, k=k: e.transpose(
            att_ps[0][:, 0:21], et_rows[0:21, k * 128:(k + 1) * 128], ident21[:]),
            deps=tuple(t_id) + (t_etl, t_etfm[-1] if t_etfm else None))
        t_cp = sched.op("dve", lambda e, k=k: e.tensor_copy(
            et_fm[:, k * 21:(k + 1) * 21], att_ps[0][:, 0:21]), deps=(t_tr,))
        t_etfm.append(t_cp)
    t_et_done = t_etfm[-1]

    def xsh(k, j):
        base = k * (N + PAD) + PAD - j
        return xT[:, base: base + N]

    def etc(k, j):
        return et_fm[:, k * 21 + j: k * 21 + j + 1]

    conv_done = []
    conv_ps = [att_ps[3][:, 0:512], att_ps[3][:, 512:1024]]
    cp_guard = [None, None]
    cpar = [0]
    for k in range(KC):
        deps0 = (t_xT[k], t_et_done)
        npe = len(PE_TAPS)
        eye_b = eye16[:].unsqueeze(1).broadcast_to([128, npe, 128])
        etv = et_fm[:, k * 21 + PE_TAPS[0]: k * 21 + PE_TAPS[0] + npe]
        et_b = etv.unsqueeze(2).broadcast_to([128, npe, 128])
        t_dg = sched.op("dve", lambda e, k=k, eye_b=eye_b, et_b=et_b: e.tensor_tensor(
            diag16[:].rearrange("p (j m) -> p j m", m=128), eye_b, et_b,
            ALU.mult), deps=(t_et_done, t_eye) +
            (tuple(conv_done[-1][1]) if conv_done else ()))
        pe_evacs = []
        for nt in range(4):
            par = cpar[0] % 2
            cpar[0] += 1

            def mk_conv(e, k=k, nt=nt, par=par):
                last = None
                for ji, j in enumerate(PE_TAPS):
                    e.ldweights(diag16[:, ji * 128:(ji + 1) * 128])
                    last = e.matmul(
                        conv_ps[par], diag16[:, ji * 128:(ji + 1) * 128],
                        xT[:, k * (N + PAD) + PAD - j + nt * 512:
                           k * (N + PAD) + PAD - j + nt * 512 + 512],
                        start=(ji == 0), stop=(ji == len(PE_TAPS) - 1))
                return last
            cdeps = (t_dg, t_xT[k])
            if cp_guard[par] is not None:
                cdeps = cdeps + (cp_guard[par],)
            t_cmm = sched.op("pe", mk_conv, deps=cdeps)
            t_evc = sched.op("act", lambda e, k=k, nt=nt, par=par: e.activation(
                hT[:, k * N + nt * 512: k * N + (nt + 1) * 512],
                conv_ps[par], AF.Identity), deps=(t_cmm,))
            cp_guard[par] = t_evc
            pe_evacs.append(t_evc)
        prev = None
        dve_buf = None
        for ji, j in enumerate(DVE_TAPS):
            dst = convA if ji % 2 == 0 else convB
            srcb = convB if ji % 2 == 0 else convA
            if ji == 0:
                prev = sched.op("dve", lambda e, k=k, j=j, dst=dst: e.tensor_scalar(
                    dst[:], xsh(k, j), etc(k, j), None, ALU.mult, ALU.bypass),
                    deps=deps0 + ((conv_done[-1][0],) if conv_done else ()))
            else:
                prev = sched.op("dve",
                                lambda e, k=k, j=j, dst=dst, srcb=srcb:
                                e.scalar_tensor_tensor(dst[:], xsh(k, j), etc(k, j),
                                                       srcb[:], ALU.mult, ALU.add),
                                deps=(prev,))
            dve_buf = dst
        dve_last = prev
        et_bc1 = etc(k, POOL_TAPS[0]).broadcast_to([128, N])
        et_bc2 = etc(k, POOL_TAPS[1]).broadcast_to([128, N])
        tp1 = sched.op("pool", lambda e, k=k, et_bc1=et_bc1: e.tensor_tensor(
            convC[:], xsh(k, POOL_TAPS[0]), et_bc1, ALU.mult),
            deps=deps0 + ((conv_done[-1][0],) if conv_done else ()))
        tp2 = sched.op("pool", lambda e, k=k, et_bc2=et_bc2: e.tensor_tensor(
            convD[:], xsh(k, POOL_TAPS[1]), et_bc2, ALU.mult), deps=(tp1,))
        tp3 = sched.op("pool", lambda e: e.tensor_tensor(
            convE[:], convC[:], convD[:], ALU.add), deps=(tp2,))
        t_h1 = sched.op("dve", lambda e, dve_buf=dve_buf: e.tensor_tensor(
            convD[:], dve_buf[:], convE[:], ALU.add), deps=(dve_last, tp3))
        t_h2 = sched.op("dve", lambda e, k=k: e.tensor_tensor(
            hT[:, k * N:(k + 1) * N], hT[:, k * N:(k + 1) * N], convD[:],
            ALU.add), deps=(t_h1,) + tuple(pe_evacs))
        conv_done.append((t_h2, pe_evacs))
    conv_done = [c[0] for c in conv_done]

    def mk_mm(e, t, ps):
        last = None
        for k in range(KC):
            stat = xT[:, k * (N + PAD) + PAD + t * 128:
                      k * (N + PAD) + PAD + (t + 1) * 128]
            e.ldweights(stat)
            e.matmul(ps[:, 0:512], stat, W1b[:, k * D: k * D + 512],
                     start=(k == 0), stop=False)
            last = e.matmul(ps[:, 512:1024], stat,
                            W1b[:, k * D + 512:(k + 1) * D],
                            start=(k == 0), stop=False)
        return last

    def mk_mm2(e, t, ps):
        for k in range(KC):
            stat = hT[:, k * N + t * 128: k * N + (t + 1) * 128]
            e.ldweights(stat)
            e.matmul(ps[:, 0:512], stat, W2b[:, k * D: k * D + 512],
                     start=False, stop=False)
            e.matmul(ps[:, 512:1024], stat, W2b[:, k * D + 512:(k + 1) * D],
                     start=False, stop=False)
        e.ldweights(onesb[0:1, :])
        e.matmul(ps[:, 0:512], onesb[0:1, :], biasb[0:1, 0:512],
                 start=False, stop=True)
        return e.matmul(ps[:, 512:1024], onesb[0:1, :], biasb[0:1, 512:1024],
                        start=False, stop=True)

    attn_evac = [None] * NT
    tok_x = [None] * NT
    ln_nm = [None] * NT
    t_y1d = [None] * NT
    ln_prev = None
    q15 = dict(qb15)
    q15["neg5"] = neg5
    q15["pos5"] = pos5
    y1q_tok = [None] * NT
    y1q_dump = [None] * NT
    qprev = {}
    prev_dump = {0: None, 1: None}

    for t in range(3):
        tok_x[t] = sched.op("pe", lambda e, t=t: mk_mm(e, t, att_ps[t % 3]),
                            deps=tuple(t_xT) + tuple(t_w1) + (t_et_done,))
    for t in range(NT):
        ps = att_ps[t % 3]
        tok_h = sched.op("pe", lambda e, t=t, ps=ps: mk_mm2(e, t, ps),
                         deps=tuple(conv_done) + tuple(t_w2) +
                         (tok_x[t], t_ones, t_bias))
        t_ev = sched.op("dve", lambda e, t=t, ps=ps: e.tensor_tensor(
            x_res[:, t * D:(t + 1) * D], ps[:, :], x_res[:, t * D:(t + 1) * D],
            ALU.add), deps=(tok_h, t_xres[t]))
        attn_evac[t] = t_ev
        if t + 3 < NT:
            tok_x[t + 3] = sched.op(
                "pe", lambda e, t=t: mk_mm(e, t + 3, att_ps[(t + 3) % 3]),
                deps=(t_ev,))
        t_bn0 = sched.op("dve", lambda e, t=t: e.bn_stats(
            bnbuf[:, 0:6], x_res[:, t * D: t * D + 512]), deps=(t_ev, ln_prev))
        t_bn = sched.op("dve", lambda e, t=t: e.bn_stats(
            bnbuf[:, 6:12], x_res[:, t * D + 512:(t + 1) * D]), deps=(t_bn0,))
        t_ag = sched.op("dve", lambda e: e.bn_aggr(mvst[:, 0:2], bnbuf[:]),
                        deps=(t_bn,))
        t_sq = sched.op("act", lambda e: e.activation(
            sqc[:], mvst[:, 1:2], AF.Sqrt, bias=epsc[:, 0:1]), deps=(t_ag,))
        t_rsd = sched.op("dve", lambda e: e.reciprocal(rstdc[:], sqc[:]),
                         deps=(t_sq,))
        t_nm0 = sched.op("dve", lambda e: e.tensor_scalar(
            nmr[:], mvst[:, 0:1], rstdc[:, 0:1], -1.0, ALU.mult, ALU.mult),
            deps=(t_rsd,))
        t_nm = sched.op("act", lambda e, t=t: e.activation(
            x_res[:, t * D:(t + 1) * D], x_res[:, t * D:(t + 1) * D],
            AF.Identity, bias=nmr[:, 0:1], scale=rstdc[:, 0:1]),
            deps=(t_nm0, t_sq))
        ln_nm[t] = t_nm
        t_y1d[t] = sched.dma("sp", y1_dram[t * 128:(t + 1) * 128, :],
                             x_res[:, t * D:(t + 1) * D], deps=(t_nm,))
        ln_prev = t_nm
        stg = y1q_st[t % 2]
        qdeps = ((prev_dump[t % 2],) if prev_dump[t % 2] else ())
        Tq = emit_quant(sched, q15, x_res[:, t * D:(t + 1) * D],
                        stg[:], QW, (t_nm,), extra_q_deps=qdeps,
                        pool_combines=True,
                        prev1=qprev.get(t - 1), prev2=qprev.get(t - 1))
        qprev[t] = Tq
        y1q_tok[t] = Tq["q"]
        y1q_dump[t] = sched.dma("sp", y1q_dram[t * 128:(t + 1) * 128, :],
                                stg[:], deps=(Tq["q"],))
        prev_dump[t % 2] = y1q_dump[t]

    y1qT_half = [None, None]
    for half in range(2):
        y1qT_half[half] = [
            sched.dma("sp", y1qT[:, k * N + half * 1024: k * N + half * 1024 + 1024],
                      y1q_dram[half * 1024:(half + 1) * 1024,
                               k * 128:(k + 1) * 128],
                      deps=tuple(y1q_dump[half * 8:(half + 1) * 8]),
                      transpose=True) for k in range(KC)]

    t_wgT = [sched.dma("sp", wgT[:, k * I:(k + 1) * I],
                       wgq_d[:, k * 128:(k + 1) * 128], transpose=True,
                       deps=(y1q_tok[-1],) + tuple(t_y1d)) for k in range(KC)]
    t_wuT = [sched.dma("sp", wuT[:, k * I:(k + 1) * I],
                       wuq_d[:, k * 128:(k + 1) * 128], transpose=True,
                       deps=(y1q_tok[-1],) + tuple(t_y1d)) for k in range(KC)]


    IBANKS = [(ib * 512, min(512, I - ib * 512)) for ib in range((I + 511) // 512)]
    hq2 = dict(qb2)
    hq2["neg5"] = neg5
    hq2["pos5"] = pos5
    t_hq_store = [None] * NT
    psum_guard = {}
    h_guard = {0: None, 1: None}
    prev_st = [None]
    hqprev = {}
    hbufs = [h_t, h_tB]
    h_parts_of = [None] * NT
    pslot = [0]

    def emit_ffn_tile(t):
        hb = hbufs[t % 2]
        h_parts = []
        for bi, (i0, w) in enumerate(IBANKS):
            ps = att_ps[pslot[0] % 4]
            sg = sg_b[pslot[0] % 4]
            uv = up_b[pslot[0] % 4]
            deps = tuple(t_wgT) + tuple(t_wuT) + tuple(y1qT_half[t // 8])
            if psum_guard.get(pslot[0] % 4) is not None:
                deps = deps + (psum_guard[pslot[0] % 4],)
            pslot[0] += 1

            def mk_gmm(e, t=t, i0=i0, w=w, ps=ps):
                last = None
                for k in range(KC):
                    stat = y1qT[:, k * N + t * 128: k * N + (t + 1) * 128]
                    e.ldweights(stat)
                    e.matmul(ps[:, 0:w], stat, wgT[:, k * I + i0: k * I + i0 + w],
                             start=(k == 0), stop=(k == KC - 1))
                    last = e.matmul(ps[:, 512:512 + w], stat,
                                    wuT[:, k * I + i0: k * I + i0 + w],
                                    start=(k == 0), stop=(k == KC - 1))
                return last
            t_mm = sched.op("pe", mk_gmm, deps=deps)
            t_sg = sched.op("act", lambda e, sg=sg, ps=ps, w=w: e.activation(
                sg[:, 0:w], ps[:, 0:w], AF.Silu), deps=(t_mm,))
            t_uv = sched.op("act", lambda e, uv=uv, ps=ps, w=w: e.activation(
                uv[:, 0:w], ps[:, 512:512 + w], AF.Identity), deps=(t_sg,))
            heng = "pool"
            t_hm = sched.op(heng,
                            lambda e, hb=hb, sg=sg, uv=uv, i0=i0, w=w:
                            e.tensor_tensor(hb[:, i0:i0 + w], sg[:, 0:w],
                                            uv[:, 0:w], ALU.mult),
                            deps=(t_uv, h_guard[t % 2]))
            psum_guard[(pslot[0] - 1) % 4] = t_hm
            h_parts.append(t_hm)
        h_parts_of[t] = h_parts

    def emit_quant_head(t):
        hb = hbufs[t % 2]
        bset = dict(hq2)
        if t % 2 == 1:
            bset["amax"], bset["s"], bset["rs"] = amaxB, sB, rsB
        Tq = emit_quant(sched, bset, hb[:], hq_t[:], I,
                        tuple(h_parts_of[t]),
                        extra_q_deps=((prev_st[0],) if prev_st[0] else ()),
                        pool_combines=False, pool_q=False, pool_u=True,
                        prev1=hqprev.get(t - 1), prev2=hqprev.get(t - 2),
                        defer_tail=True)
        hqprev[t] = Tq
        h_guard[t % 2] = Tq["u"]
        return Tq

    def emit_quant_tail(t, Tq):
        Tq["_tail"]()
        t_st = sched.dma("sp", hq_dram[t * 128:(t + 1) * 128, :],
                         hq_t[0:128, :], deps=(Tq["q"],))
        t_hq_store[t] = t_st
        prev_st[0] = t_st

    pending = [None]
    for t in range(NT):
        if t >= 1:
            pending[0] = (t - 1, emit_quant_head(t - 1))
        emit_ffn_tile(t)
        if pending[0] is not None:
            emit_quant_tail(*pending[0])
            pending[0] = None
    Tl = emit_quant_head(NT - 1)
    emit_quant_tail(NT - 1, Tl)

    t_wdT = [sched.dma("sp", wdT[:, ic * D:(ic + 1) * D],
                       wdq_d[:, ic * 128:(ic + 1) * 128], transpose=True,
                       deps=tuple(t_hq_store)) for ic in range(IC)]
    t_hqT = []
    for half in range(2):
        t_hqT.append([sched.dma(
            "sp", hqT[:, ic * N + half * 1024: ic * N + half * 1024 + 1024],
            hq_dram[half * 1024:(half + 1) * 1024, ic * 128:(ic + 1) * 128],
            transpose=True,
            deps=tuple(tk for tk in t_hq_store[half * 8:(half + 1) * 8] if tk))
            for ic in range(IC)])

    def mk_dmm(e, t, ps):
        last = None
        for ic in range(IC):
            stat = hqT[:, ic * N + t * 128: ic * N + (t + 1) * 128]
            e.ldweights(stat)
            e.matmul(ps[:, 0:512], stat, wdT[:, ic * D: ic * D + 512],
                     start=(ic == 0), stop=(ic == IC - 1))
            last = e.matmul(ps[:, 512:1024], stat,
                            wdT[:, ic * D + 512:(ic + 1) * D],
                            start=(ic == 0), stop=(ic == IC - 1))
        return last

    down_evac = [None] * NT
    t_ostores = [None] * NT
    ln2_prev = None
    for t in range(NT):
        ps = att_ps[t % 4]
        deps = tuple(t_hqT[t // 8]) + tuple(t_wdT)
        if t >= 4:
            deps = deps + (down_evac[t - 4],)
        else:
            deps = deps + (psum_guard.get(t % 4),)
        t_mm = sched.op("pe", lambda e, t=t, ps=ps: mk_dmm(e, t, ps), deps=deps)
        yslot = y1r[:, (t % 3) * D:(t % 3 + 1) * D]
        ydeps = (t_hq_store[NT - 1],)
        if t >= 3:
            ydeps = ydeps + (down_evac[t - 3],)
        t_yl = sched.dma("sp", yslot, y1_dram[t * 128:(t + 1) * 128, :],
                         deps=ydeps)
        oslot = outb[:, (t % 3) * D:(t % 3 + 1) * D]
        odeps = (t_mm, t_yl, t_hq_store[NT - 1])
        if t >= 3:
            odeps = odeps + (t_ostores[t - 3],)
        t_ev = sched.op("dve",
                        lambda e, ps=ps, yslot=yslot, oslot=oslot: e.tensor_tensor(
                            oslot, ps[:, :], yslot, ALU.add), deps=odeps)
        down_evac[t] = t_ev
        t_bn0 = sched.op("dve", lambda e, oslot=oslot: e.bn_stats(
            bnbuf[:, 0:6], oslot[:, 0:512]), deps=(t_ev, ln2_prev))
        t_bn = sched.op("dve", lambda e, oslot=oslot: e.bn_stats(
            bnbuf[:, 6:12], oslot[:, 512:1024]), deps=(t_bn0,))
        t_ag = sched.op("dve", lambda e: e.bn_aggr(mvst[:, 2:4], bnbuf[:]),
                        deps=(t_bn,))
        t_sq = sched.op("act", lambda e: e.activation(
            sqc[:], mvst[:, 3:4], AF.Sqrt, bias=epsc[:, 0:1]), deps=(t_ag,))
        t_rsd = sched.op("dve", lambda e: e.reciprocal(rstdc[:], sqc[:]),
                         deps=(t_sq,))
        t_nm0 = sched.op("dve", lambda e: e.tensor_scalar(
            nmr[:], mvst[:, 2:3], rstdc[:, 0:1], -1.0, ALU.mult, ALU.mult),
            deps=(t_rsd,))
        t_nm = sched.op("act", lambda e, oslot=oslot: e.activation(
            oslot, oslot, AF.Identity, bias=nmr[:, 0:1], scale=rstdc[:, 0:1]),
            deps=(t_nm0, t_sq))
        t_ostores[t] = sched.dma("sp", out_d[t * 128:(t + 1) * 128, :],
                                 oslot, deps=(t_nm,))
        ln2_prev = t_nm

    finals = sched.all_dma_barrier("sp")
    sched.finalize()
    with es, nc.Block() as block:
        @block.sync
        def _(sync):
            sched.replay("sp", sync)
            for (sname, v) in finals:
                sync.wait_ge(sched.sems[sname], v)

        @block.vector
        def _(v):
            sched.replay("dve", v)

        @block.scalar
        def _(s):
            sched.replay("act", s)

        @block.gpsimd
        def _(g):
            sched.replay("pool", g)

        @block.tensor
        def _(t):
            sched.replay("pe", t)
    return nc


_NC_A = None
_NC_B = None


def kernel(L0_attn_input_data, L0_attn_W1, L0_attn_W2, L0_attn_bias,
           L0_attn_embeddings_table, L0_ffn_w_gate, L0_ffn_w_up,
           L0_ffn_w_down):
    global _NC_A, _NC_B
    x = np.ascontiguousarray(np.asarray(L0_attn_input_data, dtype=np.float32))
    W1 = np.ascontiguousarray(np.asarray(L0_attn_W1, dtype=np.float32))
    W2 = np.ascontiguousarray(np.asarray(L0_attn_W2, dtype=np.float32))
    bias = np.ascontiguousarray(
        np.asarray(L0_attn_bias, dtype=np.float32)).reshape(1, D)
    et = np.ascontiguousarray(
        np.asarray(L0_attn_embeddings_table, dtype=np.float32))
    wg = np.ascontiguousarray(np.asarray(L0_ffn_w_gate, dtype=np.float32))
    wu = np.ascontiguousarray(np.asarray(L0_ffn_w_up, dtype=np.float32))
    wd = np.ascontiguousarray(np.asarray(L0_ffn_w_down, dtype=np.float32))

    if _NC_A is None:
        _NC_A = build_launch_a()
    WG_R, WD_R = I // NCORES, D // NCORES
    in_a = [{"wg": wg[c * WG_R:(c + 1) * WG_R],
             "wu": wu[c * WG_R:(c + 1) * WG_R],
             "wd": wd[c * WD_R:(c + 1) * WD_R]} for c in range(NCORES)]
    res_a = run_bass_kernel_spmd(_NC_A, in_a, core_ids=list(range(NCORES)))
    wgq = np.concatenate([res_a.results[c]["wgq"] for c in range(NCORES)], axis=0)
    wuq = np.concatenate([res_a.results[c]["wuq"] for c in range(NCORES)], axis=0)
    wdq = np.concatenate([res_a.results[c]["wdq"] for c in range(NCORES)], axis=0)

    if _NC_B is None:
        _NC_B = build_launch_b()
    ident = np.eye(21, dtype=np.float32)
    eye128 = np.eye(128, dtype=np.float32)
    in_b = [{"x": x[c], "w1": W1, "w2": W2, "bias": bias, "et": et,
             "ident": ident, "eye128": eye128,
             "wgq": wgq, "wuq": wuq, "wdq": wdq} for c in range(NCORES)]
    res_b = run_bass_kernel_spmd(_NC_B, in_b, core_ids=list(range(NCORES)))
    out = np.stack([res_b.results[c]["out"] for c in range(NCORES)], axis=0)
    return out.astype(np.float32)


# version 11
# speedup vs baseline: 1.1498x; 1.0157x over previous
import numpy as np
import concourse.bass as bass
import concourse.mybir as mybir
from concourse.bass_utils import run_bass_kernel_spmd
from contextlib import ExitStack

F32 = mybir.dt.float32
BF16 = mybir.dt.bfloat16
FP16 = mybir.dt.float16
ALU = mybir.AluOpType
AF = mybir.ActivationFunctionType
AX = mybir.AxisListType

NCORES = 8
N, D, I = 2048, 1024, 2816
NT = N // 128
KC = D // 128
IC = I // 128
RND_C = 12582912.0
RND_HC = 6291456.0
PE_TAPS = list(range(1, 15))
DVE_TAPS = list(range(15, 19))
POOL_TAPS = list(range(19, 21))
PAD = 32
QW = 1024
DEBUG_TAPS = False


class Sched:

    ENGS = ("pe", "dve", "act", "pool", "sp")
    NPOOL = {"act": 12, "pool": 8, "sp": 20}

    def __init__(self, nc):
        self.nc = nc
        self.items = {e: [] for e in self.ENGS}
        self.count = {}
        self.sems = {}
        self.rot = {e: 0 for e in self.NPOOL}

    def make_sems(self, es):
        for e in self.ENGS:
            self.sems[e] = es.enter_context(self.nc.semaphore(f"sem_{e}"))
            self.count[e] = 0
        for e, np_ in self.NPOOL.items():
            for i in range(np_):
                q = f"{e}q{i}"
                self.sems[q] = es.enter_context(self.nc.semaphore(f"sem_{q}"))
                self.count[q] = 0

    def op(self, eng, fn, deps=()):
        self.count[eng] += 1
        tok = (eng, self.count[eng])
        self.items[eng].append(
            (fn, tuple(d for d in deps if d is not None), tok, 1, eng))
        return tok

    def dma(self, eng, out, in_, deps=(), **kw):
        assert eng in self.NPOOL
        q = f"{eng}q{self.rot[eng]}"
        self.rot[eng] = (self.rot[eng] + 1) % self.NPOOL[eng]
        prev = self.count[q]
        deps = tuple(d for d in deps if d is not None)
        if prev > 0:
            deps = deps + ((q, prev),)
        self.count[q] += 16
        tok = (q, self.count[q])
        self.items[eng].append(
            (lambda e: e.dma_start(out=out, in_=in_, **kw), deps, tok, 16, q))
        return tok

    def finalize(self):
        self.prior = {}
        for eng in self.ENGS:
            run = {}
            for (fn, deps, tok, inc, sem) in self.items[eng]:
                self.prior[tok] = dict(run)
                run[sem] = tok[1]

    def replay(self, eng, e):
        waited = {}
        for (fn, deps, tok, inc, sem) in self.items[eng]:
            want = {}
            for (s, v) in deps:
                if s == eng:
                    v = min(v, self.prior[tok].get(s, 0))
                if v > want.get(s, 0):
                    want[s] = v
            for s, v in want.items():
                if v > waited.get(s, 0):
                    e.wait_ge(self.sems[s], v)
                    waited[s] = v
            inst = fn(e)
            if inst is not None:
                inst.then_inc(self.sems[sem], inc)

    def barrier_token(self, eng, dma=False):
        return (eng, self.count[eng])

    def all_dma_barrier(self, eng):
        return tuple((f"{eng}q{i}", self.count[f"{eng}q{i}"])
                     for i in range(self.NPOOL[eng])
                     if self.count[f"{eng}q{i}"] > 0)


class Arena:
    def __init__(self, nc, total_bytes):
        self.nc = nc
        self.total = total_bytes
        self.handle = None
        self.named = []

    def reserve(self, es):
        self.handle = es.enter_context(
            self.nc.sbuf_tensor("arena", [128, self.total], mybir.dt.uint8))

    def at(self, name, offset, shape, dtype):
        per_part = int(np.prod(shape[1:])) * mybir.dt.size(dtype)
        assert offset % 32 == 0, (name, offset)
        assert offset + per_part <= self.total, (name, offset, per_part)
        h = self.nc.virtual_sbuf_tensor(name, shape, dtype)
        self.named.append((h, offset))
        return h

    def finalize(self):
        base = self.nc.lookup_mloc(self.handle).addr
        for h, off in self.named:
            self.nc._tensor(h.name, h.shape, h.dtype, type="SB")
            mloc = self.nc.lookup_mloc(h)
            mloc.allocated = not self.nc.target_bir_lowering
            mloc.addr = base + off


def emit_quant(sched, bufs, src_ap, q_out_ap, width, src_tok, extra_q_deps=(),
               pool_combines=False, pool_q=None, prev1=None, prev2=None,
               defer_tail=False, pool_u=False):
    nb = width // 16
    amax, s_t, rs_t = bufs["amax"], bufs["s"], bufs["rs"]
    u, b5 = bufs["u"], bufs["b5"]
    b1, b2, b3, b4 = bufs["b1"], bufs["b2"], bufs["b3"], bufs["b4"]
    s1v, s2v = bufs["s1v"], bufs["s2v"]
    ceng = "pool" if pool_combines else "dve"
    qeng = ceng if pool_q is None else ("pool" if pool_q else "dve")
    p1 = prev1 or {}
    p2 = prev2 or {}

    def g(d, *keys):
        return tuple(d[k] for k in keys if d.get(k) is not None)

    src_deps = tuple(src_tok) if isinstance(src_tok, (tuple, list)) and \
        src_tok and isinstance(src_tok[0], tuple) else (src_tok,)
    T = {}
    src3 = src_ap.rearrange("p (b k) -> p b k", k=16)
    T["am"] = sched.op("dve", lambda e: e.tensor_reduce(
        amax[:, :nb], src3, AX.X, ALU.max, apply_absolute_value=True),
        deps=src_deps + g(p2, "s"))
    T["s"] = sched.op("dve", lambda e: e.tensor_scalar(
        s_t[:, :nb], amax[:, :nb], 1.0 / 6.0, 1e-12, ALU.mult, ALU.max),
        deps=(T["am"],) + g(p2, "q"))
    T["rs"] = sched.op("dve", lambda e: e.reciprocal(rs_t[:, :nb], s_t[:, :nb]),
                       deps=(T["s"],) + g(p2, "u"))
    rs_b = rs_t[:, :nb].unsqueeze(2).broadcast_to([128, nb, 16])
    s_b = s_t[:, :nb].unsqueeze(2).broadcast_to([128, nb, 16])
    u3 = u[:, :width].rearrange("p (b k) -> p b k", k=16)
    T["u"] = sched.op("pool" if pool_u else "dve",
                      lambda e: e.tensor_tensor(u3, src3, rs_b, ALU.mult),
                      deps=(T["rs"],) + g(p1, "r", "s2"))
    uf = u[:, :width]
    T["rh"] = sched.op("dve", lambda e: e.tensor_scalar(
        b5[:, :width], uf, RND_HC, RND_HC, ALU.add, ALU.subtract),
        deps=(T["u"],) + g(p1, "a2", "c2"))
    T["a2"] = sched.op("dve", lambda e: e.tensor_scalar(
        b1[:, :width], b5[:, :width], -2.0, 2.0, ALU.max, ALU.min),
        deps=(T["rh"],) + g(p1, "g", "q"))
    T["r"] = sched.op("dve", lambda e: e.tensor_scalar(
        b5[:, :width], uf, RND_C, RND_C, ALU.add, ALU.subtract),
        deps=(T["a2"],))
    T["b1"] = sched.op("dve", lambda e: e.tensor_scalar(
        b2[:, :width], b5[:, :width], -4.0, 4.0, ALU.max, ALU.min),
        deps=(T["r"],) + g(p1, "c1", "c3"))
    T["b2"] = sched.op("dve", lambda e: e.tensor_scalar(
        b3[:, :width], b5[:, :width], -2.0, 2.0, ALU.max, ALU.min),
        deps=(T["b1"],) + g(p1, "c1", "c2", "g"))
    T["s1"] = sched.op("act", lambda e: e.activation(
        s1v[:, :width], uf, AF.Sign, bias=bufs["neg5"][:, 0:1]),
        deps=(T["u"], T["b2"]) + g(p1, "c2"))
    T["s2"] = sched.op("act", lambda e: e.activation(
        s2v[:, :width], uf, AF.Sign, bias=bufs["pos5"][:, 0:1]),
        deps=(T["s1"],))
    def emit_tail():
        T["c1"] = sched.op(ceng, lambda e: e.tensor_tensor(
            b2[:, :width], b2[:, :width], b3[:, :width], ALU.subtract),
            deps=(T["b2"],))
        T["c2"] = sched.op(ceng, lambda e: e.tensor_tensor(
            b4[:, :width], s1v[:, :width], s2v[:, :width], ALU.add),
            deps=(T["s2"], T["c1"]) + g(p1, "g"))
        T["c3"] = sched.op("dve", lambda e: e.tensor_tensor(
            b1[:, :width], b1[:, :width], b2[:, :width], ALU.add),
            deps=(T["c1"], T["a2"]))
        T["g"] = sched.op("dve", lambda e: e.tensor_tensor(
            b1[:, :width], b1[:, :width], b4[:, :width], ALU.add),
            deps=(T["c3"], T["c2"]))
        q3 = q_out_ap.rearrange("p (b k) -> p b k", k=16)
        g3 = b1[:, :width].rearrange("p (b k) -> p b k", k=16)
        T["q"] = sched.op(qeng, lambda e: e.tensor_tensor(q3, g3, s_b, ALU.mult),
                          deps=(T["g"],) + tuple(extra_q_deps))
        return T
    if defer_tail:
        T["_tail"] = emit_tail
        return T
    emit_tail()
    return T


def make_quant_bufs(ar, base, width, tag):
    cur = base
    out = {}
    for nm, dt_ in (("u", F32), ("b5", F32)):
        out[nm] = ar.at(f"{tag}_{nm}", cur, [128, width], dt_)
        cur += width * 4
    for nm in ("b1", "b2", "b3", "b4", "s1v", "s2v"):
        out[nm] = ar.at(f"{tag}_{nm}", cur, [128, width], BF16)
        cur += width * 2
    for nm in ("amax", "s", "rs"):
        out[nm] = ar.at(f"{tag}_{nm}", cur, [128, width // 16], F32)
        cur += max(32, width // 16 * 4)
        cur = (cur + 31) // 32 * 32
    return out, cur


def build_launch_a():
    nc = bass.Bass("TRN2", target_bir_lowering=False, debug=False)
    WG_R = I // NCORES
    WD_R = D // NCORES
    wg_d = nc.dram_tensor("wg", [WG_R, D], F32, kind="ExternalInput").ap()
    wu_d = nc.dram_tensor("wu", [WG_R, D], F32, kind="ExternalInput").ap()
    wd_d = nc.dram_tensor("wd", [WD_R, I], F32, kind="ExternalInput").ap()
    wgq_d = nc.dram_tensor("wgq", [WG_R, D], BF16, kind="ExternalOutput").ap()
    wuq_d = nc.dram_tensor("wuq", [WG_R, D], BF16, kind="ExternalOutput").ap()
    wdq_d = nc.dram_tensor("wdq", [WD_R, I], BF16, kind="ExternalOutput").ap()

    es = ExitStack()
    sched = Sched(nc)
    sched.make_sems(es)
    W = 3072
    ar = Arena(nc, 200 * 1024)
    ar.reserve(es)
    cur = 0

    def alloc(name, shape, dt_):
        nonlocal cur
        h = ar.at(name, cur, shape, dt_)
        cur += (int(np.prod(shape[1:])) * mybir.dt.size(dt_) + 31) // 32 * 32
        return h
    src = [alloc("srcA", [128, W], F32), alloc("srcB", [128, W], F32)]
    qout = [alloc("qoutA", [128, W], BF16), alloc("qoutB", [128, W], BF16)]
    bsets = []
    for p in range(2):
        bufs = {}
        for nm, dt_ in (("u", F32), ("b5", F32)):
            bufs[nm] = alloc(f"q{p}_{nm}", [128, W], dt_)
        for nm in ("b1", "b2", "b3", "b4", "s1v", "s2v"):
            bufs[nm] = alloc(f"q{p}_{nm}", [128, W], BF16)
        for nm in ("amax", "s", "rs"):
            bufs[nm] = alloc(f"q{p}_{nm}", [128, W // 16], F32)
        bsets.append(bufs)
    neg5 = alloc("neg5", [128, 1], F32)
    pos5 = alloc("pos5", [128, 1], F32)
    for p in range(2):
        bsets[p]["neg5"] = neg5
        bsets[p]["pos5"] = pos5
    ar.finalize()

    t0 = [sched.op("dve", lambda e, p=p: e.memset(src[p][:], 0.0))
          for p in range(2)]
    t_n5 = sched.op("dve", lambda e: e.memset(neg5[:], -5.0))
    t_p5 = sched.op("dve", lambda e: e.memset(pos5[:], 5.0))

    jobs = []
    for w_in, w_out in ((wg_d, wgq_d), (wu_d, wuq_d)):
        loads = [(w_in[0:128, :], (0, 128, 0)),
                 (w_in[128:256, :], (0, 128, 1024)),
                 (w_in[256:352, :], (0, 96, 2048))]
        stores = [(w_out[0:128, :], (0, 128, 0)),
                  (w_out[128:256, :], (0, 128, 1024)),
                  (w_out[256:352, :], (0, 96, 2048))]
        jobs.append((loads, stores, 3072))
    jobs.append(([(wd_d[:], (0, 128, 0))], [(wdq_d[:], (0, 128, 0))], 2816))

    last = {0: (t0[0], t_n5, t_p5), 1: (t0[1],)}
    prevq = {0: None, 1: None}
    for ji, (loads, stores, width) in enumerate(jobs):
        p = ji % 2
        t_lds = []
        for iap, (r0, r1, c0) in loads:
            t_lds.append(sched.dma("sp", src[p][r0:r1, c0:c0 + 1024
                                               if width == 3072 else c0 + 2816],
                                   iap, deps=last[p]))
        T = emit_quant(sched, bsets[p], src[p][:, 0:width],
                       qout[p][:, 0:width], width, tuple(t_lds),
                       prev1=prevq[p], prev2=prevq[p])
        t_q = T["q"]
        t_sts = []
        for oap, (r0, r1, c0) in stores:
            t_sts.append(sched.dma("sp", oap,
                                   qout[p][r0:r1, c0:c0 + 1024
                                           if width == 3072 else c0 + 2816],
                                   deps=(t_q,)))
        last[p] = tuple(t_sts) + (t_q,)
        prevq[p] = T

    finals = sched.all_dma_barrier("sp")
    sched.finalize()
    with es, nc.Block() as block:
        @block.sync
        def _(sync):
            sched.replay("sp", sync)
            for (sname, v) in finals:
                sync.wait_ge(sched.sems[sname], v)

        @block.vector
        def _(v):
            sched.replay("dve", v)

        @block.scalar
        def _(s):
            sched.replay("act", s)

        @block.gpsimd
        def _(g):
            sched.replay("pool", g)

        @block.tensor
        def _(t):
            sched.replay("pe", t)
    return nc


def build_launch_b():
    nc = bass.Bass("TRN2", target_bir_lowering=False, debug=False)
    x_d = nc.dram_tensor("x", [N, D], F32, kind="ExternalInput").ap()
    w1_d = nc.dram_tensor("w1", [D, D], F32, kind="ExternalInput").ap()
    w2_d = nc.dram_tensor("w2", [D, D], F32, kind="ExternalInput").ap()
    bias_d = nc.dram_tensor("bias", [1, D], F32, kind="ExternalInput").ap()
    et_d = nc.dram_tensor("et", [21, D], F32, kind="ExternalInput").ap()
    id_d = nc.dram_tensor("ident", [21, 21], F32, kind="ExternalInput").ap()
    eye_d = nc.dram_tensor("eye128", [128, 128], F32, kind="ExternalInput").ap()
    wgq_d = nc.dram_tensor("wgq", [I, D], BF16, kind="ExternalInput").ap()
    wuq_d = nc.dram_tensor("wuq", [I, D], BF16, kind="ExternalInput").ap()
    wdq_d = nc.dram_tensor("wdq", [D, I], BF16, kind="ExternalInput").ap()
    out_d = nc.dram_tensor("out", [N, D], F32, kind="ExternalOutput").ap()

    _ik = "ExternalOutput" if DEBUG_TAPS else "Internal"
    xb_dram = nc.dram_tensor("xb_dram", [N, D], FP16, kind="Internal").ap()
    y1_dram = nc.dram_tensor("y1_dram", [N, D], F32, kind=_ik).ap()
    y1q_dram = nc.dram_tensor("y1q_dram", [N, D], BF16, kind=_ik).ap()
    hq_dram = nc.dram_tensor("hq_dram", [N, I], BF16, kind=_ik).ap()

    es = ExitStack()
    sched = Sched(nc)
    sched.make_sems(es)

    AR = 207 * 1024
    ar = Arena(nc, AR)
    ar.reserve(es)

    y1qT = ar.at("y1qT", 0, [128, KC * N], BF16)
    convA = ar.at("convA", 0, [128, N], FP16)
    convB = ar.at("convB", 4096, [128, N], FP16)
    convC = ar.at("convC", 8192, [128, N], FP16)
    convD = ar.at("convD", 12288, [128, N], FP16)
    convE = ar.at("convE", 16384, [128, N], FP16)
    et_rows = ar.at("et_rows", 20480, [21, D], F32)
    ident21 = ar.at("ident21", 24576, [21, 21], F32)

    GEN = 32768
    x_res = ar.at("x_res", GEN, [128, NT * D], F32)
    xT = ar.at("xT", GEN + 65536, [128, KC * (N + PAD)], FP16)
    hT = ar.at("hT", GEN + 98816, [128, KC * N], FP16)
    W1b = ar.at("W1b", GEN + 131584, [128, KC * D], FP16)
    W2b = ar.at("W2b", GEN + 148096, [128, KC * D], FP16)

    qb15 = {
        "u": ar.at("q15_u", 197248, [128, QW], F32),
        "b5": ar.at("q15_b5", 201344, [128, QW], F32),
        "amax": ar.at("q15_amax", 205440, [128, QW // 16], F32),
        "s": ar.at("q15_s", 205728, [128, QW // 16], F32),
        "rs": ar.at("q15_rs", 206016, [128, QW // 16], F32),
        "b1": ar.at("q15_b1", 2048, [128, QW], BF16),
        "b2": ar.at("q15_b2", 6144, [128, QW], BF16),
        "b3": ar.at("q15_b3", 10240, [128, QW], BF16),
        "b4": ar.at("q15_b4", 14336, [128, QW], BF16),
        "s1v": ar.at("q15_s1v", 18432, [128, QW], BF16),
        "s2v": ar.at("q15_s2v", 22528, [128, QW], BF16),
    }
    y1q_st = [ar.at("y1qs0", 26624, [128, D], BF16),
              ar.at("y1qs1", 30720, [128, D], BF16)]
    diag16 = ar.at("diag16", 197248, [128, len(PE_TAPS) * 128], FP16)
    amaxB = ar.at("amaxB", 197248, [128, I // 16], F32)
    sB = ar.at("sB", 197984, [128, I // 16], F32)
    rsB = ar.at("rsB", 198720, [128, I // 16], F32)

    wgT = ar.at("wgT", 32768, [128, KC * I], BF16)
    wuT = ar.at("wuT", 77824, [128, KC * I], BF16)
    qb2 = {
        "u": ar.at("q2_u", 122880, [128, I], F32),
        "b5": ar.at("q2_b5", 134144, [128, I], F32),
        "s1v": ar.at("q2_s1v", 134144, [128, I], BF16),
        "s2v": ar.at("q2_s2v", 139776, [128, I], BF16),
        "b1": ar.at("q2_b1", 145408, [128, I], BF16),
        "b2": ar.at("q2_b2", 151040, [128, I], BF16),
        "b3": ar.at("q2_b3", 156672, [128, I], BF16),
        "b4": ar.at("q2_b4", 156672, [128, I], BF16),
        "amax": ar.at("q2_amax", 162304, [128, I // 16], F32),
        "s": ar.at("q2_s", 163040, [128, I // 16], F32),
        "rs": ar.at("q2_rs", 163776, [128, I // 16], F32),
    }
    h_t = ar.at("h_t", 164512, [128, I], F32)
    h_tB = ar.at("h_tB", 175776, [128, I], F32)
    hq_t = ar.at("hq_t", 187040, [128, I], BF16)
    sg_b = [None] * 4
    up_b = [None] * 4
    cur2 = 199456
    for i in range(4):
        sg_b[i] = ar.at(f"sg{i}", cur2, [128, 512], BF16); cur2 += 1024
        up_b[i] = ar.at(f"up{i}", cur2, [128, 512], BF16); cur2 += 1024
    assert cur2 <= 208160, cur2

    hqT = ar.at("hqT", 32768, [128, IC * N], BF16)
    wdT = ar.at("wdT", 122880, [128, IC * D], BF16)
    y1r = ar.at("y1r", 167936, [128, 3 * D], F32)
    outb = ar.at("outb", 180224, [128, 3 * D], F32)

    sm = 208160
    et_fm = ar.at("et_fm", sm, [128, KC * 21], F32); sm += KC * 21 * 4
    sm = (sm + 31) // 32 * 32
    biasb = ar.at("biasb", sm, [1, D], BF16); sm += D * 2
    onesb = ar.at("onesb", sm, [1, 128], BF16); sm += 256
    neg5 = ar.at("neg5", sm, [128, 1], F32); sm += 32
    pos5 = ar.at("pos5", sm, [128, 1], F32); sm += 32
    epsc = ar.at("epsc", sm, [128, 1], F32); sm += 32
    mvst = ar.at("mvst", sm, [128, 8], F32); sm += 32
    bnbuf = ar.at("bnbuf", sm, [128, 12], F32); sm += 64
    rstdc = ar.at("rstdc", sm, [128, 1], F32); sm += 32
    sqc = ar.at("sqc", sm, [128, 1], F32); sm += 32
    nmr = ar.at("nmr", sm, [128, 1], F32); sm += 32
    eye16 = ar.at("eye16", sm, [128, 128], FP16); sm += 256
    assert sm <= AR, sm

    att_ps = [es.enter_context(nc.psum_tensor(f"aps{i}", [128, 1024], F32))
              for i in range(4)]

    ar.finalize()

    t_n5 = sched.op("dve", lambda e: e.memset(neg5[:], -5.0))
    t_p5 = sched.op("dve", lambda e: e.memset(pos5[:], 5.0))
    t_ep = sched.op("dve", lambda e: e.memset(epsc[:], 1e-5))
    t_ones = sched.op("dve", lambda e: e.memset(onesb[:], 1.0))
    t_pad = [sched.op("dve", lambda e, k=k: e.memset(
        xT[:, k * (N + PAD): k * (N + PAD) + PAD], 0.0)) for k in range(KC)]

    t_xres = [sched.dma("sp", x_res[:, t * D:(t + 1) * D],
                        x_d[t * 128:(t + 1) * 128, :]) for t in range(NT)]
    t_xb = sched.dma("pool", xb_dram[:], x_d[:])
    t_xT = [sched.dma("sp", xT[:, k * (N + PAD) + PAD:(k + 1) * (N + PAD)],
                      xb_dram[:, k * 128:(k + 1) * 128],
                      deps=(t_xb, t_pad[k]), transpose=True) for k in range(KC)]
    t_w1 = [sched.dma("pool", W1b[:, k * D:(k + 1) * D],
                      w1_d[k * 128:(k + 1) * 128, :]) for k in range(KC)]
    t_w2 = [sched.dma("pool", W2b[:, k * D:(k + 1) * D],
                      w2_d[k * 128:(k + 1) * 128, :]) for k in range(KC)]
    t_bias = sched.dma("pool", biasb[:], bias_d[:])
    t_eye = sched.dma("pool", eye16[:], eye_d[:])
    t_etl = sched.dma("sp", et_rows[:], et_d[:])
    t_id = [sched.dma("sp", ident21[:], id_d[:])]
    t_etfm = []
    for k in range(KC):
        t_tr = sched.op("pe", lambda e, k=k: e.transpose(
            att_ps[0][:, 0:21], et_rows[0:21, k * 128:(k + 1) * 128], ident21[:]),
            deps=tuple(t_id) + (t_etl, t_etfm[-1] if t_etfm else None))
        t_cp = sched.op("dve", lambda e, k=k: e.tensor_copy(
            et_fm[:, k * 21:(k + 1) * 21], att_ps[0][:, 0:21]), deps=(t_tr,))
        t_etfm.append(t_cp)
    t_et_done = t_etfm[-1]

    def xsh(k, j):
        base = k * (N + PAD) + PAD - j
        return xT[:, base: base + N]

    def etc(k, j):
        return et_fm[:, k * 21 + j: k * 21 + j + 1]

    conv_done = []
    conv_ps = [att_ps[3][:, 0:512], att_ps[3][:, 512:1024]]
    cp_guard = [None, None]
    cpar = [0]
    for k in range(KC):
        deps0 = (t_xT[k], t_et_done)
        npe = len(PE_TAPS)
        eye_b = eye16[:].unsqueeze(1).broadcast_to([128, npe, 128])
        etv = et_fm[:, k * 21 + PE_TAPS[0]: k * 21 + PE_TAPS[0] + npe]
        et_b = etv.unsqueeze(2).broadcast_to([128, npe, 128])
        t_dg = sched.op("dve", lambda e, k=k, eye_b=eye_b, et_b=et_b: e.tensor_tensor(
            diag16[:].rearrange("p (j m) -> p j m", m=128), eye_b, et_b,
            ALU.mult), deps=(t_et_done, t_eye) +
            (tuple(conv_done[-1][1]) if conv_done else ()))
        pe_evacs = []
        for nt in range(4):
            par = cpar[0] % 2
            cpar[0] += 1

            def mk_conv(e, k=k, nt=nt, par=par):
                last = None
                for ji, j in enumerate(PE_TAPS):
                    e.ldweights(diag16[:, ji * 128:(ji + 1) * 128])
                    last = e.matmul(
                        conv_ps[par], diag16[:, ji * 128:(ji + 1) * 128],
                        xT[:, k * (N + PAD) + PAD - j + nt * 512:
                           k * (N + PAD) + PAD - j + nt * 512 + 512],
                        start=(ji == 0), stop=(ji == len(PE_TAPS) - 1))
                return last
            cdeps = (t_dg, t_xT[k])
            if cp_guard[par] is not None:
                cdeps = cdeps + (cp_guard[par],)
            t_cmm = sched.op("pe", mk_conv, deps=cdeps)
            t_evc = sched.op("act", lambda e, k=k, nt=nt, par=par: e.activation(
                hT[:, k * N + nt * 512: k * N + (nt + 1) * 512],
                conv_ps[par], AF.Identity), deps=(t_cmm,))
            cp_guard[par] = t_evc
            pe_evacs.append(t_evc)
        prev = None
        dve_buf = None
        for ji, j in enumerate(DVE_TAPS):
            dst = convA if ji % 2 == 0 else convB
            srcb = convB if ji % 2 == 0 else convA
            if ji == 0:
                prev = sched.op("dve", lambda e, k=k, j=j, dst=dst: e.tensor_scalar(
                    dst[:], xsh(k, j), etc(k, j), None, ALU.mult, ALU.bypass),
                    deps=deps0 + ((conv_done[-1][0],) if conv_done else ()))
            else:
                prev = sched.op("dve",
                                lambda e, k=k, j=j, dst=dst, srcb=srcb:
                                e.scalar_tensor_tensor(dst[:], xsh(k, j), etc(k, j),
                                                       srcb[:], ALU.mult, ALU.add),
                                deps=(prev,))
            dve_buf = dst
        dve_last = prev
        et_bc1 = etc(k, POOL_TAPS[0]).broadcast_to([128, N])
        et_bc2 = etc(k, POOL_TAPS[1]).broadcast_to([128, N])
        tp1 = sched.op("pool", lambda e, k=k, et_bc1=et_bc1: e.tensor_tensor(
            convC[:], xsh(k, POOL_TAPS[0]), et_bc1, ALU.mult),
            deps=deps0 + ((conv_done[-1][0],) if conv_done else ()))
        tp2 = sched.op("pool", lambda e, k=k, et_bc2=et_bc2: e.tensor_tensor(
            convD[:], xsh(k, POOL_TAPS[1]), et_bc2, ALU.mult), deps=(tp1,))
        tp3 = sched.op("pool", lambda e: e.tensor_tensor(
            convE[:], convC[:], convD[:], ALU.add), deps=(tp2,))
        t_h1 = sched.op("dve", lambda e, dve_buf=dve_buf: e.tensor_tensor(
            convD[:], dve_buf[:], convE[:], ALU.add), deps=(dve_last, tp3))
        t_h2 = sched.op("dve", lambda e, k=k: e.tensor_tensor(
            hT[:, k * N:(k + 1) * N], hT[:, k * N:(k + 1) * N], convD[:],
            ALU.add), deps=(t_h1,) + tuple(pe_evacs))
        conv_done.append((t_h2, pe_evacs))
    conv_done = [c[0] for c in conv_done]

    def mk_mm(e, t, ps):
        last = None
        for k in range(KC):
            stat = xT[:, k * (N + PAD) + PAD + t * 128:
                      k * (N + PAD) + PAD + (t + 1) * 128]
            e.ldweights(stat)
            e.matmul(ps[:, 0:512], stat, W1b[:, k * D: k * D + 512],
                     start=(k == 0), stop=False)
            last = e.matmul(ps[:, 512:1024], stat,
                            W1b[:, k * D + 512:(k + 1) * D],
                            start=(k == 0), stop=False)
        return last

    def mk_mm2(e, t, ps):
        for k in range(KC):
            stat = hT[:, k * N + t * 128: k * N + (t + 1) * 128]
            e.ldweights(stat)
            e.matmul(ps[:, 0:512], stat, W2b[:, k * D: k * D + 512],
                     start=False, stop=False)
            e.matmul(ps[:, 512:1024], stat, W2b[:, k * D + 512:(k + 1) * D],
                     start=False, stop=False)
        e.ldweights(onesb[0:1, :])
        e.matmul(ps[:, 0:512], onesb[0:1, :], biasb[0:1, 0:512],
                 start=False, stop=True)
        return e.matmul(ps[:, 512:1024], onesb[0:1, :], biasb[0:1, 512:1024],
                        start=False, stop=True)

    attn_evac = [None] * NT
    tok_x = [None] * NT
    ln_nm = [None] * NT
    t_y1d = [None] * NT
    ln_prev = None
    q15 = dict(qb15)
    q15["neg5"] = neg5
    q15["pos5"] = pos5
    y1q_tok = [None] * NT
    y1q_dump = [None] * NT
    qprev = {}
    prev_dump = {0: None, 1: None}

    for t in range(3):
        tok_x[t] = sched.op("pe", lambda e, t=t: mk_mm(e, t, att_ps[t % 3]),
                            deps=tuple(t_xT) + tuple(t_w1) + (t_et_done,))
    for t in range(NT):
        ps = att_ps[t % 3]
        tok_h = sched.op("pe", lambda e, t=t, ps=ps: mk_mm2(e, t, ps),
                         deps=tuple(conv_done) + tuple(t_w2) +
                         (tok_x[t], t_ones, t_bias))
        t_ev = sched.op("dve", lambda e, t=t, ps=ps: e.tensor_tensor(
            x_res[:, t * D:(t + 1) * D], ps[:, :], x_res[:, t * D:(t + 1) * D],
            ALU.add), deps=(tok_h, t_xres[t]))
        attn_evac[t] = t_ev
        if t + 3 < NT:
            tok_x[t + 3] = sched.op(
                "pe", lambda e, t=t: mk_mm(e, t + 3, att_ps[(t + 3) % 3]),
                deps=(t_ev,))
        t_bn0 = sched.op("dve", lambda e, t=t: e.bn_stats(
            bnbuf[:, 0:6], x_res[:, t * D: t * D + 512]), deps=(t_ev, ln_prev))
        t_bn = sched.op("dve", lambda e, t=t: e.bn_stats(
            bnbuf[:, 6:12], x_res[:, t * D + 512:(t + 1) * D]), deps=(t_bn0,))
        t_ag = sched.op("dve", lambda e: e.bn_aggr(mvst[:, 0:2], bnbuf[:]),
                        deps=(t_bn,))
        t_sq = sched.op("act", lambda e: e.activation(
            sqc[:], mvst[:, 1:2], AF.Sqrt, bias=epsc[:, 0:1]), deps=(t_ag,))
        t_rsd = sched.op("dve", lambda e: e.reciprocal(rstdc[:], sqc[:]),
                         deps=(t_sq,))
        t_nm0 = sched.op("dve", lambda e: e.tensor_scalar(
            nmr[:], mvst[:, 0:1], rstdc[:, 0:1], -1.0, ALU.mult, ALU.mult),
            deps=(t_rsd,))
        t_nm = sched.op("act", lambda e, t=t: e.activation(
            x_res[:, t * D:(t + 1) * D], x_res[:, t * D:(t + 1) * D],
            AF.Identity, bias=nmr[:, 0:1], scale=rstdc[:, 0:1]),
            deps=(t_nm0, t_sq))
        ln_nm[t] = t_nm
        t_y1d[t] = sched.dma("sp", y1_dram[t * 128:(t + 1) * 128, :],
                             x_res[:, t * D:(t + 1) * D], deps=(t_nm,))
        ln_prev = t_nm
        stg = y1q_st[t % 2]
        qdeps = ((prev_dump[t % 2],) if prev_dump[t % 2] else ())
        Tq = emit_quant(sched, q15, x_res[:, t * D:(t + 1) * D],
                        stg[:], QW, (t_nm,), extra_q_deps=qdeps,
                        pool_combines=True,
                        prev1=qprev.get(t - 1), prev2=qprev.get(t - 1))
        qprev[t] = Tq
        y1q_tok[t] = Tq["q"]
        y1q_dump[t] = sched.dma("sp", y1q_dram[t * 128:(t + 1) * 128, :],
                                stg[:], deps=(Tq["q"],))
        prev_dump[t % 2] = y1q_dump[t]

    y1qT_half = [None, None]
    for half in range(2):
        y1qT_half[half] = [
            sched.dma("sp", y1qT[:, k * N + half * 1024: k * N + half * 1024 + 1024],
                      y1q_dram[half * 1024:(half + 1) * 1024,
                               k * 128:(k + 1) * 128],
                      deps=tuple(y1q_dump[half * 8:(half + 1) * 8]),
                      transpose=True) for k in range(KC)]

    def xdead(byte0, byte1):
        t0 = max(0, (byte0 - 32768) // 4096)
        t1 = min(NT - 1, (byte1 - 1 - 32768) // 4096)
        dd = []
        for tt in range(t0, t1 + 1):
            dd += [y1q_tok[tt], t_y1d[tt]]
        return tuple(dd)
    t_wgT = [sched.dma("sp", wgT[:, k * I:(k + 1) * I],
                       wgq_d[:, k * 128:(k + 1) * 128], transpose=True,
                       deps=xdead(32768 + k * 5632, 32768 + (k + 1) * 5632))
             for k in range(KC)]
    t_wuT = [sched.dma("sp", wuT[:, k * I:(k + 1) * I],
                       wuq_d[:, k * 128:(k + 1) * 128], transpose=True,
                       deps=xdead(77824 + k * 5632, 77824 + (k + 1) * 5632) +
                       ((attn_evac[NT - 1],) if k >= 4 else ()))
             for k in range(KC)]


    IBANKS = [(ib * 512, min(512, I - ib * 512)) for ib in range((I + 511) // 512)]
    hq2 = dict(qb2)
    hq2["neg5"] = neg5
    hq2["pos5"] = pos5
    t_hq_store = [None] * NT
    psum_guard = {}
    h_guard = {0: None, 1: None}
    prev_st = [None]
    hqprev = {}
    hbufs = [h_t, h_tB]
    h_parts_of = [None] * NT
    pslot = [0]

    def emit_ffn_tile(t):
        hb = hbufs[t % 2]
        h_parts = []
        for bi, (i0, w) in enumerate(IBANKS):
            ps = att_ps[pslot[0] % 4]
            sg = sg_b[pslot[0] % 4]
            uv = up_b[pslot[0] % 4]
            deps = tuple(t_wgT) + tuple(t_wuT) + tuple(y1qT_half[t // 8])
            if psum_guard.get(pslot[0] % 4) is not None:
                deps = deps + (psum_guard[pslot[0] % 4],)
            pslot[0] += 1

            def mk_gmm(e, t=t, i0=i0, w=w, ps=ps):
                last = None
                for k in range(KC):
                    stat = y1qT[:, k * N + t * 128: k * N + (t + 1) * 128]
                    e.ldweights(stat)
                    e.matmul(ps[:, 0:w], stat, wgT[:, k * I + i0: k * I + i0 + w],
                             start=(k == 0), stop=(k == KC - 1))
                    last = e.matmul(ps[:, 512:512 + w], stat,
                                    wuT[:, k * I + i0: k * I + i0 + w],
                                    start=(k == 0), stop=(k == KC - 1))
                return last
            t_mm = sched.op("pe", mk_gmm, deps=deps)
            t_sg = sched.op("act", lambda e, sg=sg, ps=ps, w=w: e.activation(
                sg[:, 0:w], ps[:, 0:w], AF.Silu), deps=(t_mm,))
            t_uv = sched.op("act", lambda e, uv=uv, ps=ps, w=w: e.activation(
                uv[:, 0:w], ps[:, 512:512 + w], AF.Identity), deps=(t_sg,))
            heng = "pool"
            t_hm = sched.op(heng,
                            lambda e, hb=hb, sg=sg, uv=uv, i0=i0, w=w:
                            e.tensor_tensor(hb[:, i0:i0 + w], sg[:, 0:w],
                                            uv[:, 0:w], ALU.mult),
                            deps=(t_uv, h_guard[t % 2]))
            psum_guard[(pslot[0] - 1) % 4] = t_hm
            h_parts.append(t_hm)
        h_parts_of[t] = h_parts

    def emit_quant_head(t):
        hb = hbufs[t % 2]
        bset = dict(hq2)
        if t % 2 == 1:
            bset["amax"], bset["s"], bset["rs"] = amaxB, sB, rsB
        Tq = emit_quant(sched, bset, hb[:], hq_t[:], I,
                        tuple(h_parts_of[t]),
                        extra_q_deps=((prev_st[0],) if prev_st[0] else ()),
                        pool_combines=False, pool_q=False, pool_u=False,
                        prev1=hqprev.get(t - 1), prev2=hqprev.get(t - 2),
                        defer_tail=True)
        hqprev[t] = Tq
        h_guard[t % 2] = Tq["u"]
        return Tq

    def emit_quant_tail(t, Tq):
        Tq["_tail"]()
        t_st = sched.dma("sp", hq_dram[t * 128:(t + 1) * 128, :],
                         hq_t[0:128, :], deps=(Tq["q"],))
        t_hq_store[t] = t_st
        prev_st[0] = t_st

    pending = [None]
    for t in range(NT):
        if t >= 1:
            pending[0] = (t - 1, emit_quant_head(t - 1))
        emit_ffn_tile(t)
        if pending[0] is not None:
            emit_quant_tail(*pending[0])
            pending[0] = None
    Tl = emit_quant_head(NT - 1)
    emit_quant_tail(NT - 1, Tl)

    t_wdT = [sched.dma("sp", wdT[:, ic * D:(ic + 1) * D],
                       wdq_d[:, ic * 128:(ic + 1) * 128], transpose=True,
                       deps=tuple(t_hq_store)) for ic in range(IC)]
    t_hqT = []
    for half in range(2):
        t_hqT.append([sched.dma(
            "sp", hqT[:, ic * N + half * 1024: ic * N + half * 1024 + 1024],
            hq_dram[half * 1024:(half + 1) * 1024, ic * 128:(ic + 1) * 128],
            transpose=True,
            deps=tuple(tk for tk in t_hq_store[half * 8:(half + 1) * 8] if tk))
            for ic in range(IC)])

    def mk_dmm(e, t, ps, ic0, ic1):
        last = None
        for ic in range(ic0, ic1):
            stat = hqT[:, ic * N + t * 128: ic * N + (t + 1) * 128]
            e.ldweights(stat)
            e.matmul(ps[:, 0:512], stat, wdT[:, ic * D: ic * D + 512],
                     start=(ic == 0), stop=(ic == IC - 1))
            last = e.matmul(ps[:, 512:1024], stat,
                            wdT[:, ic * D + 512:(ic + 1) * D],
                            start=(ic == 0), stop=(ic == IC - 1))
        return last

    down_evac = [None] * NT
    t_ostores = [None] * NT
    ln2_prev = None
    for t in range(NT):
        ps = att_ps[t % 4]
        deps = tuple(t_hqT[t // 8]) + tuple(t_wdT)
        if t >= 4:
            deps = deps + (down_evac[t - 4],)
        else:
            deps = deps + (psum_guard.get(t % 4),)
        if t < 4:
            hh = t // 8
            dep1 = tuple(t_hqT[hh][0:11]) + tuple(t_wdT[0:11])
            if t >= 4:
                dep1 = dep1 + (down_evac[t - 4],)
            else:
                dep1 = dep1 + (psum_guard.get(t % 4),)
            sched.op("pe", lambda e, t=t, ps=ps: mk_dmm(e, t, ps, 0, 11),
                     deps=dep1)
            t_mm = sched.op("pe", lambda e, t=t, ps=ps: mk_dmm(e, t, ps, 11, IC),
                            deps=tuple(t_hqT[hh][11:]) + tuple(t_wdT[11:]))
        else:
            t_mm = sched.op("pe", lambda e, t=t, ps=ps: mk_dmm(e, t, ps, 0, IC),
                            deps=deps)
        yslot = y1r[:, (t % 3) * D:(t % 3 + 1) * D]
        ydeps = (t_hq_store[NT - 1],)
        if t >= 3:
            ydeps = ydeps + (down_evac[t - 3],)
        t_yl = sched.dma("sp", yslot, y1_dram[t * 128:(t + 1) * 128, :],
                         deps=ydeps)
        oslot = outb[:, (t % 3) * D:(t % 3 + 1) * D]
        odeps = (t_mm, t_yl, t_hq_store[NT - 1])
        if t >= 3:
            odeps = odeps + (t_ostores[t - 3],)
        t_ev = sched.op("dve",
                        lambda e, ps=ps, yslot=yslot, oslot=oslot: e.tensor_tensor(
                            oslot, ps[:, :], yslot, ALU.add), deps=odeps)
        down_evac[t] = t_ev
        t_bn0 = sched.op("dve", lambda e, oslot=oslot: e.bn_stats(
            bnbuf[:, 0:6], oslot[:, 0:512]), deps=(t_ev, ln2_prev))
        t_bn = sched.op("dve", lambda e, oslot=oslot: e.bn_stats(
            bnbuf[:, 6:12], oslot[:, 512:1024]), deps=(t_bn0,))
        t_ag = sched.op("dve", lambda e: e.bn_aggr(mvst[:, 2:4], bnbuf[:]),
                        deps=(t_bn,))
        t_sq = sched.op("act", lambda e: e.activation(
            sqc[:], mvst[:, 3:4], AF.Sqrt, bias=epsc[:, 0:1]), deps=(t_ag,))
        t_rsd = sched.op("dve", lambda e: e.reciprocal(rstdc[:], sqc[:]),
                         deps=(t_sq,))
        t_nm0 = sched.op("dve", lambda e: e.tensor_scalar(
            nmr[:], mvst[:, 2:3], rstdc[:, 0:1], -1.0, ALU.mult, ALU.mult),
            deps=(t_rsd,))
        t_nm = sched.op("act", lambda e, oslot=oslot: e.activation(
            oslot, oslot, AF.Identity, bias=nmr[:, 0:1], scale=rstdc[:, 0:1]),
            deps=(t_nm0, t_sq))
        t_ostores[t] = sched.dma("sp", out_d[t * 128:(t + 1) * 128, :],
                                 oslot, deps=(t_nm,))
        ln2_prev = t_nm

    finals = sched.all_dma_barrier("sp")
    sched.finalize()
    with es, nc.Block() as block:
        @block.sync
        def _(sync):
            sched.replay("sp", sync)
            for (sname, v) in finals:
                sync.wait_ge(sched.sems[sname], v)

        @block.vector
        def _(v):
            sched.replay("dve", v)

        @block.scalar
        def _(s):
            sched.replay("act", s)

        @block.gpsimd
        def _(g):
            sched.replay("pool", g)

        @block.tensor
        def _(t):
            sched.replay("pe", t)
    return nc


_NC_A = None
_NC_B = None


def kernel(L0_attn_input_data, L0_attn_W1, L0_attn_W2, L0_attn_bias,
           L0_attn_embeddings_table, L0_ffn_w_gate, L0_ffn_w_up,
           L0_ffn_w_down):
    global _NC_A, _NC_B
    x = np.ascontiguousarray(np.asarray(L0_attn_input_data, dtype=np.float32))
    W1 = np.ascontiguousarray(np.asarray(L0_attn_W1, dtype=np.float32))
    W2 = np.ascontiguousarray(np.asarray(L0_attn_W2, dtype=np.float32))
    bias = np.ascontiguousarray(
        np.asarray(L0_attn_bias, dtype=np.float32)).reshape(1, D)
    et = np.ascontiguousarray(
        np.asarray(L0_attn_embeddings_table, dtype=np.float32))
    wg = np.ascontiguousarray(np.asarray(L0_ffn_w_gate, dtype=np.float32))
    wu = np.ascontiguousarray(np.asarray(L0_ffn_w_up, dtype=np.float32))
    wd = np.ascontiguousarray(np.asarray(L0_ffn_w_down, dtype=np.float32))

    if _NC_A is None:
        _NC_A = build_launch_a()
    WG_R, WD_R = I // NCORES, D // NCORES
    in_a = [{"wg": wg[c * WG_R:(c + 1) * WG_R],
             "wu": wu[c * WG_R:(c + 1) * WG_R],
             "wd": wd[c * WD_R:(c + 1) * WD_R]} for c in range(NCORES)]
    res_a = run_bass_kernel_spmd(_NC_A, in_a, core_ids=list(range(NCORES)))
    wgq = np.concatenate([res_a.results[c]["wgq"] for c in range(NCORES)], axis=0)
    wuq = np.concatenate([res_a.results[c]["wuq"] for c in range(NCORES)], axis=0)
    wdq = np.concatenate([res_a.results[c]["wdq"] for c in range(NCORES)], axis=0)

    if _NC_B is None:
        _NC_B = build_launch_b()
    ident = np.eye(21, dtype=np.float32)
    eye128 = np.eye(128, dtype=np.float32)
    in_b = [{"x": x[c], "w1": W1, "w2": W2, "bias": bias, "et": et,
             "ident": ident, "eye128": eye128,
             "wgq": wgq, "wuq": wuq, "wdq": wdq} for c in range(NCORES)]
    res_b = run_bass_kernel_spmd(_NC_B, in_b, core_ids=list(range(NCORES)))
    out = np.stack([res_b.results[c]["out"] for c in range(NCORES)], axis=0)
    return out.astype(np.float32)
